# version 25
# baseline (speedup 1.0000x reference)
import numpy as np
import ml_dtypes
from contextlib import ExitStack

import concourse.bass as bass
import concourse.mybir as mybir
from concourse.bass_utils import run_bass_kernel_spmd

F32 = mybir.dt.float32
BF16 = mybir.dt.bfloat16
AF = mybir.ActivationFunctionType
ALU = mybir.AluOpType
AX = mybir.AxisListType

D = 1024
T = 2048
NCH = 8
HS = 2048
HR = 1024
E = 8
EPS_LN = 1e-5
EPS_RMS = 1.1920929e-07


class Op:
    __slots__ = ("eng", "fn", "deps", "dma", "chan", "fence", "needed", "sem", "tick")

    def __init__(self, eng, fn, deps, dma, chan=None, fence=False):
        self.eng, self.fn, self.dma, self.chan = eng, fn, dma, chan
        self.fence = fence
        self.deps = [d for d in deps if d is not None]
        self.needed = False
        self.sem = None
        self.tick = None


class Rec:

    ENGINES = ("pe", "act", "dve", "pool", "sync")

    def __init__(self, nc, es):
        self.nc = nc
        self.es = es
        self.ops = []
        self.sems = {e: es.enter_context(nc.semaphore("s_" + e)) for e in self.ENGINES}
        self.dsems = {}

    def op(self, eng, fn, deps=(), dma=False, chan=None, fence=False):
        o = Op(eng, fn, deps, dma, chan, fence)
        self.ops.append(o)
        return o

    def pe(self, fn, deps=()):
        return self.op("pe", fn, deps)

    def act(self, fn, deps=()):
        return self.op("act", fn, deps)

    def dve(self, fn, deps=(), fence=False):
        return self.op("dve", fn, deps, fence=fence)

    def pool(self, fn, deps=()):
        return self.op("pool", fn, deps)

    def dma(self, queue, fn, deps=(), chan="d0"):
        if chan not in self.dsems:
            self.dsems[chan] = self.es.enter_context(self.nc.semaphore("dm_" + chan))
        return self.op(queue, fn, deps, dma=True, chan=chan)

    def emit(self, blk):
        for o in self.ops:
            if o.dma:
                o.needed = True
            for d in o.deps:
                if d.eng == o.eng and not d.dma and not o.fence:
                    continue
                d.needed = True
        cnt = {}
        for o in self.ops:
            if not o.needed:
                continue
            if o.dma:
                o.sem = self.dsems[o.chan]
                key = ("d", o.chan)
                cnt[key] = cnt.get(key, 0) + 16
            else:
                o.sem = self.sems[o.eng]
                key = ("c", o.eng)
                cnt[key] = cnt.get(key, 0) + 1
            o.tick = cnt[key]

        per_eng = {e: [o for o in self.ops if o.eng == e] for e in self.ENGINES}

        def body(name):
            def run(eng):
                waited = {}
                for o in per_eng[name]:
                    for d in o.deps:
                        if d.eng == o.eng and not d.dma and not o.fence:
                            continue
                        if waited.get(d.sem, 0) < d.tick:
                            eng.wait_ge(d.sem, d.tick)
                            waited[d.sem] = d.tick
                    instr = o.fn(eng)
                    if o.needed:
                        instr.then_inc(o.sem, 16 if o.dma else 1)
            return run

        blk.tensor(body("pe"))
        blk.scalar(body("act"))
        blk.vector(body("dve"))
        blk.gpsimd(body("pool"))
        blk.sync(body("sync"))


def _copy(eng, out, in_):
    if eng.engine == mybir.EngineType.Activation:
        return eng.copy(out, in_)
    return eng.tensor_copy(out, in_)


def build_program(stages=(1, 2, 3)):
    nc = bass.Bass("TRN2", target_bir_lowering=False)
    es = ExitStack()

    def din(name, shape, dtype=F32):
        return nc.dram_tensor(name, list(shape), dtype, kind="ExternalInput").ap()

    x_d = din("x", (D, T))
    sw1_d = din("sw1", (D, HS))
    sw3_d = din("sw3", (D, HS))
    sw2_d = din("sw2", (HS, D))
    w1_d = din("w1", (E, D, HR))
    w2_d = din("w2", (E, HR, D))
    sb1c_d = din("sb1c", (128, 16))
    sb3c_d = din("sb3c", (128, 16))
    sb2r_d = din("sb2r", (1, D))
    b1c_d = din("b1c", (128, 64))
    b2r_d = din("b2r", (1, E * D))
    wdq_d = din("wdq", (512, 64))
    wuqt_d = din("wuqt", (512, 64))
    wdkv_d = din("wdkv", (1024, 64))
    wukvt_d = din("wukvt", (1024, 128))
    wqrt_d = din("wqrt", (64, 64))
    wkrt_d = din("wkrt", (128, 64))
    wot_d = din("wot", (128, 64))
    knw2_d = din("knw2", (2, 128))
    qnw2_d = din("qnw2", (2, 128))
    identb_d = din("identb", (128, 128), BF16)
    identf_d = din("identf", (128, 128), F32)
    b2ones_d = din("b2ones", (128, 2), BF16)
    onesr_d = din("onesr", (33, 512), BF16)
    onesc_d = din("onesc", (128, 1), BF16)
    epsln_d = din("epsln", (1, 1), F32)
    epsr2_d = din("epsr2", (2, 1), F32)
    out_d = nc.dram_tensor("out", [D, T], F32, kind="ExternalOutput").ap()
    sw1s_d = nc.dram_tensor("s_w1", [32, 128, 2048], BF16, kind="Internal").ap()
    sw2s_d = nc.dram_tensor("s_w2", [8, 128, 8192], BF16, kind="Internal").ap()

    def sbt(name, shape, dtype):
        return es.enter_context(nc.sbuf_tensor("sb_" + name, list(shape), dtype))

    X = sbt("X", (128, NCH * T), F32)
    XH = sbt("XH", (128, NCH * T), BF16)

    PK1 = sbt("PK1", (65, 1024), F32)
    PK2 = sbt("PK2", (33, 1024), F32)
    PKB = sbt("PKB", (33, 1024), BF16)
    m_row = PK1[0:1, :]
    ms2_row = PK1[32:33, :]
    b_row = PK1[64:65, :]
    a_row = PK2[0:1, :]
    var_row = PK2[32:33, :]
    abf_row = PKB[0:1, :]
    bbf_row = PKB[32:33, :]

    sb1c = sbt("sb1c", (128, 16), F32)
    sb3c = sbt("sb3c", (128, 16), F32)
    b1c = sbt("b1c", (128, 64), F32)
    sb2r = sbt("sb2r", (1, D), BF16)
    mxc = sbt("mxc", (128, 8), F32)
    rs = sbt("rs", (128, 8), F32)
    bqk = sbt("bqk", (128, 12), F32)
    mbf = sbt("mbf", (1, 2), BF16)
    lnc = sbt("lnc", (2, 1), BF16)
    negon = sbt("negon", (1, 128), BF16)

    wdq = sbt("wdq", (128, 4 * 64), BF16)
    wuqt = sbt("wuqt", (128, 4 * 64), BF16)
    wdkv = sbt("wdkv", (128, 8 * 64), BF16)
    wukvt = sbt("wukvt", (128, 8 * 128), BF16)
    wqrt = sbt("wqrt", (64, 64), BF16)
    wkrt = sbt("wkrt", (128, 64), BF16)
    wot = sbt("wot", (128, 64), BF16)
    wqt = sbt("wqt", (128, 64), BF16)
    wkvt = sbt("wkvt", (128, 128), BF16)
    knw2 = sbt("knw2", (2, 128), F32)
    qnw2 = sbt("qnw2", (2, 128), F32)
    knw2b = sbt("knw2b", (2, 128), BF16)
    qnw2b = sbt("qnw2b", (2, 128), BF16)
    identb = sbt("identb", (128, 128), BF16)
    identf = sbt("identf", (128, 128), F32)
    b2ones = sbt("b2ones", (128, 2), BF16)
    onesr = sbt("onesr", (33, 512), BF16)
    onesc = sbt("onesc", (128, 1), BF16)
    epsln = sbt("epsln", (1, 1), F32)
    epsr2 = sbt("epsr2", (2, 1), F32)

    AW = 21504
    ARENA = sbt("ARENA", (128, AW), F32)

    def av_bf(lo, hi, parts=128):
        return ARENA[0:parts, lo:hi].bitcast(BF16)

    def av_f(lo, hi, parts=128):
        return ARENA[0:parts, lo:hi]

    SW2 = av_bf(0, 8192)
    Gh = av_bf(8192, 16384)
    SW1s = [av_bf(16384, 16896), av_bf(16896, 17408)]
    SW3s = [av_bf(17408, 17920), av_bf(17920, 18432)]
    T1b = [av_bf(18432, 18688), av_bf(18688, 18944)]
    T3b = [av_bf(18944, 19200), av_bf(19200, 19456)]
    W1q = [av_bf(19456, 20480), av_bf(20480, 21504)]
    XP = av_bf(0, 8192)
    Hs2 = av_bf(8192, 10240)
    b2s = [av_bf(10240, 10752, 1), av_bf(10752, 11264, 1)]
    W2s = [av_bf(11264, 15360), av_bf(15360, 19456)]
    XBF2 = [av_bf(8192, 10240), av_bf(10240, 12288)]
    XSQ2 = [av_bf(12288, 14336), av_bf(14336, 16384)]
    Asb2 = [sbt("Asb0", (128, 512), BF16), sbt("Asb1", (128, 512), BF16)]
    Bsb2 = [sbt("Bsb0", (128, 512), BF16), sbt("Bsb1", (128, 512), BF16)]
    TMUL = sbt("TMUL", (128, 512), BF16)
    K2 = [av_bf(0, 2048), av_bf(2048, 4096)]
    Q2 = [av_bf(4096, 6144), av_bf(6144, 8192)]
    V2 = [av_bf(8192, 9280), av_bf(9280, 10368)]
    Psb = av_bf(10368, 11392)
    PTs = av_bf(11392, 12416)
    kvsb = av_bf(12416, 12928)
    qcsb = av_bf(12928, 13440, 64)
    sqk = av_bf(13440, 13696)
    sqq = av_bf(13696, 13952)
    row2 = av_f(13952, 14976, 2)
    rstd2 = av_bf(14976, 15488, 2)
    P3 = av_bf(15488, 16000)
    SQA = av_bf(16384, 17408)
    SQB = av_bf(17408, 18432)
    AO2 = [av_bf(18432, 19456), av_bf(19456, 20480)]

    PS_A = es.enter_context(nc.psum_tensor("PS_A", [128, 1024], F32))
    PS_B = es.enter_context(nc.psum_tensor("PS_B", [128, 1024], F32))
    PS_C = es.enter_context(nc.psum_tensor("PS_C", [128, 1024], F32))
    PS_D = es.enter_context(nc.psum_tensor("PS_D", [128, 1024], F32))
    PTbf = PS_C[:, 0:512].bitcast(BF16)
    SAB = [PS_A, PS_B]

    r = Rec(nc, es)

    ld_x = []
    for h2 in range(2):
        ld_x.append(r.dma(("sync", "act")[h2], lambda e, h2_=h2: e.dma_start(
            out=X.rearrange("p (c t) -> p c t", t=T)[:, h2_ * 4:(h2_ + 1) * 4, :],
            in_=x_d.rearrange("(c p) t -> p c t", p=128)[:, h2_ * 4:(h2_ + 1) * 4, :]),
            chan="xio%d" % h2))

    small = []

    def gload(dst, src):
        o = r.dma("pool", lambda e, d=dst, s=src: e.dma_start(out=d, in_=s), chan="const")
        small.append(o)
        return o

    for dst, src in [(sb1c[:], sb1c_d[:]), (sb3c[:], sb3c_d[:]), (b1c[:], b1c_d[:]),
                     (sb2r[:], sb2r_d[:]),
                     (identb[:], identb_d[:]), (identf[:], identf_d[:]),
                     (b2ones[:], b2ones_d[:]), (onesr[:], onesr_d[:]), (onesc[:], onesc_d[:]),
                     (epsln[:], epsln_d[:]), (epsr2[:], epsr2_d[:]),
                     (wqrt[:], wqrt_d[:]), (wkrt[:], wkrt_d[:]), (wot[:], wot_d[:])]:
        gload(dst, src)
    ld_knw2 = gload(knw2[:], knw2_d[:])
    ld_qnw2 = gload(qnw2[:], qnw2_d[:])
    ld_wdq = gload(wdq.rearrange("p (kc d) -> p kc d", d=64),
                   wdq_d.rearrange("(kc p) d -> p kc d", p=128))
    ld_wuqt = gload(wuqt.rearrange("p (kc d) -> p kc d", d=64),
                    wuqt_d.rearrange("(kc p) d -> p kc d", p=128))
    ld_wdkv = gload(wdkv.rearrange("p (kc d) -> p kc d", d=64),
                    wdkv_d.rearrange("(kc p) d -> p kc d", p=128))
    ld_wukvt = gload(wukvt.rearrange("p (kc d) -> p kc d", d=128),
                     wukvt_d.rearrange("(kc p) d -> p kc d", p=128))
    consts = [small[-1]]

    def fold_q(e):
        for kc in range(4):
            i = e.matmul(PS_D[0:64, 0:64], wdq[:, kc * 64:(kc + 1) * 64],
                         wuqt[:, kc * 64:(kc + 1) * 64], start=(kc == 0), stop=(kc == 3))
        return i
    o_foldq = r.pe(fold_q, deps=consts)

    def fold_kv(e):
        for kc in range(8):
            i = e.matmul(PS_D[0:64, 512:640], wdkv[:, kc * 64:(kc + 1) * 64],
                         wukvt[:, kc * 128:(kc + 1) * 128], start=(kc == 0), stop=(kc == 7))
        return i
    o_foldkv = r.pe(fold_kv, deps=consts)

    def wqt_ev(e):
        e.copy(wqt[0:64, :], PS_D[0:64, 0:64])
        return e.copy(wqt[64:128, :], PS_D[0:64, 0:64])
    o_wqt = r.act(wqt_ev, deps=[o_foldq])

    def wkvt_ev(e):
        e.copy(wkvt[0:64, :], PS_D[0:64, 512:640])
        return e.copy(wkvt[64:128, :], PS_D[0:64, 512:640])
    o_wkvt = r.act(wkvt_ev, deps=[o_foldkv])
    o_qnw = r.dve(lambda e: e.tensor_scalar_mul(qnw2b[:], qnw2[:], 0.125), deps=consts)
    o_knw = r.dve(lambda e: e.tensor_copy(knw2b[:], knw2[:]), deps=consts)
    o_negon = r.pool(lambda e: e.memset(negon[:], -1.0))
    attn_wready = [o_wqt, o_wkvt, o_qnw, o_knw, o_negon]

    LN_SUM = PS_C[0:1, 0:512]
    LN_SUMSQ = PS_C[0:1, 512:1024]
    LN_A = PS_A[:, 0:512]
    LN_B = PS_A[:, 512:1024]

    def emit_ln(start_deps):
        group_done = []
        prev_stats = {0: None, 1: None}
        prev_apply = {0: [], 1: []}
        prev_rowrd = {0: None, 1: None}
        prev_evac = {0: None, 1: None}
        stage2 = {}
        for g in range(5):
          if g < 4:
            gofs = g * 512
            gp = g % 2
            ro = gp * 512
            XBF = XBF2[gp]
            XSQ = XSQ2[gp]
            Asb = Asb2[gp]
            Bsb = Bsb2[gp]
            SUM = PS_C[0:1, 0:512] if gp == 0 else PS_D[0:1, 0:512]
            SUMSQ = PS_C[0:1, 512:1024] if gp == 0 else PS_D[0:1, 512:1024]
            BA = PS_A[:, 0:512] if gp == 0 else PS_B[:, 0:512]
            BB = PS_A[:, 512:1024] if gp == 0 else PS_B[:, 512:1024]
            casts = []
            sqs = []
            for c in range(NCH):
                xin = X[:, c * T + gofs: c * T + gofs + 512]
                xb = XBF[:, c * 512:(c + 1) * 512]
                xs = XSQ[:, c * 512:(c + 1) * 512]
                war = [prev_stats[gp]] if prev_stats[gp] else []
                eng = ("dve", "act")[c % 2]
                o_c = r.op(eng, lambda e, a_=xb, b_=xin: _copy(e, a_, b_),
                           deps=list(start_deps) + war)
                o_s = r.dve(lambda e, a_=xs, b_=xb: e.tensor_mul(a_, b_, b_),
                            deps=[o_c] + war)
                casts.append(o_c)
                sqs.append(o_s)

            def stats(e, XBF_=XBF, XSQ_=XSQ, SUM_=SUM, SUMSQ_=SUMSQ):
                for c in range(NCH):
                    e.matmul(SUM_, onesc[:], XBF_[:, c * 512:(c + 1) * 512],
                             start=(c == 0), stop=(c == NCH - 1))
                for c in range(NCH):
                    i = e.matmul(SUMSQ_, onesc[:], XSQ_[:, c * 512:(c + 1) * 512],
                                 start=(c == 0), stop=(c == NCH - 1))
                return i
            o_stats = r.pe(stats, deps=casts + sqs +
                           ([prev_rowrd[gp]] if prev_rowrd[gp] else []) + list(start_deps))
            prev_stats[gp] = o_stats

            stage2[g] = (gp, ro, XBF, Asb, Bsb, BA, BB, SUM, SUMSQ, o_stats, casts, gofs)
          if g - 1 >= 0:
            (gp, ro, XBF, Asb, Bsb, BA, BB, SUM, SUMSQ, o_stats, casts, gofs) = stage2[g - 1]
            o_m = r.act(lambda e, S_=SUM, ro_=ro: e.activation(
                m_row[0:1, ro_:ro_ + 512], S_, AF.Copy, scale=1.0 / D), deps=[o_stats])
            o_m2 = r.act(lambda e, S_=SUMSQ, ro_=ro: e.activation(
                ms2_row[0:1, ro_:ro_ + 512], S_, AF.Copy, scale=1.0 / D), deps=[o_stats])
            prev_rowrd[gp] = o_m2
            o_v1 = r.dve(lambda e, ro_=ro: e.tensor_mul(
                var_row[0:1, ro_:ro_ + 512], m_row[0:1, ro_:ro_ + 512],
                m_row[0:1, ro_:ro_ + 512]), deps=[o_m])
            o_v2 = r.dve(lambda e, ro_=ro: e.tensor_sub(
                var_row[0:1, ro_:ro_ + 512], ms2_row[0:1, ro_:ro_ + 512],
                var_row[0:1, ro_:ro_ + 512]), deps=[o_v1, o_m2], fence=True)
            o_ln = r.act(lambda e, ro_=ro: e.activation(
                a_row[0:1, ro_:ro_ + 512], var_row[0:1, ro_:ro_ + 512],
                AF.Ln, bias=epsln[:]), deps=[o_v2])
            o_ex = r.act(lambda e, ro_=ro: e.activation(
                a_row[0:1, ro_:ro_ + 512], a_row[0:1, ro_:ro_ + 512],
                AF.Exp, scale=-0.5), deps=[o_ln])
            o_b = r.dve(lambda e, ro_=ro: e.scalar_tensor_tensor(
                b_row[0:1, ro_:ro_ + 512], m_row[0:1, ro_:ro_ + 512], -1.0,
                a_row[0:1, ro_:ro_ + 512], ALU.mult, ALU.mult), deps=[o_ex])
            o_ab = r.dve(lambda e, ro_=ro: e.tensor_copy(
                abf_row[0:1, ro_:ro_ + 512], a_row[0:1, ro_:ro_ + 512]), deps=[o_ex])
            o_bb = r.dve(lambda e, ro_=ro: e.tensor_copy(
                bbf_row[0:1, ro_:ro_ + 512], b_row[0:1, ro_:ro_ + 512]),
                deps=[o_b], fence=True)

            def bcast(e, BA_=BA, BB_=BB, ro_=ro):
                e.matmul(BA_, onesr[0:1, 0:128], abf_row[0:1, ro_:ro_ + 512],
                         start=True, stop=True)
                return e.matmul(BB_, onesr[32:33, 0:128], bbf_row[0:1, ro_:ro_ + 512],
                                start=True, stop=True)
            o_bc = r.pe(bcast, deps=[o_ab, o_bb] +
                        ([prev_evac[gp]] if prev_evac[gp] else []))
            o_Ae = r.act(lambda e, A_=Asb, BA_=BA: e.copy(A_[:], BA_),
                         deps=[o_bc] + prev_apply[gp][-1:])
            o_Be = r.act(lambda e, B_=Bsb, BB_=BB: e.copy(B_[:], BB_),
                         deps=[o_bc] + prev_apply[gp][-1:])
            prev_evac[gp] = o_Be

            applies = []
            for c in range(NCH):
                xin = X[:, c * T + gofs: c * T + gofs + 512]
                xh = XH[:, c * T + gofs: c * T + gofs + 512]
                o_t = r.dve(lambda e, a_=xin, A_=Asb: e.tensor_mul(TMUL[:], a_, A_[:]),
                            deps=[o_Ae])
                o_a = r.dve(lambda e, dd=xh, B_=Bsb: e.tensor_add(dd, TMUL[:], B_[:]),
                            deps=[o_t, o_Be])
                applies.append(o_a)
            prev_apply[gp] = applies
            group_done.append(applies)
        tails = [prev_stats[0], prev_stats[1], prev_evac[0], prev_evac[1]] + \
            group_done[-1][-2:] + group_done[-2][-2:]
        return group_done, tails

    RES = []
    for e_ in range(E):
        r0, r1 = (3 * e_) % 8, (3 * e_ + 3) % 8
        RES.append((min(r0, r1), abs(r1 - r0)))

    UB = [[PS_A[:, 0:512], PS_A[:, 512:1024], PS_B[:, 0:512], PS_B[:, 512:1024]],
          [PS_C[:, 0:512], PS_C[:, 512:1024], PS_D[:, 0:512], PS_D[:, 512:1024]]]
    HB4 = [PS_A[:, 0:512], PS_A[:, 512:1024], PS_B[:, 0:512], PS_B[:, 512:1024]]
    YB2 = [PS_C[:, 0:512], PS_C[:, 512:1024]]

    def xp_rhs(c, a_, d_):
        return XP[:, c * T:(c + 1) * T].rearrange(
            "p (gh r o) -> p gh r o", gh=2, r=8)[:, :, a_:a_ + d_ + 1:d_, :]

    def x_scat(c, gh, a_, d_):
        base = c * T + gh * 1024
        return X[:, base: base + 1024].rearrange(
            "p (o i) -> p o i", i=8)[:, :, a_:a_ + d_ + 1:d_].transpose([0, 2, 1])

    scr_sts = {}

    def emit_ffn(xh_groups, start_deps, dma_deps, sw2_deps=None, scratch=None):
        ld_sw2_holder = []

        def rec_sw2():
            ld_sw2_holder.append(r.dma("pool", lambda e: e.dma_start(
                out=SW2.rearrange("p (hc c) -> p hc c", c=1024),
                in_=sw2_d.rearrange("(hc p) c -> p hc c", p=128)),
                deps=(sw2_deps if sw2_deps is not None else dma_deps), chan="sw2"))

        up_pe = {}
        glue = {}
        g_last = {}
        down_war = []
        down_done = []
        all_gel = []

        for half in range(2):
            hofs = half * 1024
            xh_dep = xh_groups[half * 2] + xh_groups[half * 2 + 1]
            g_ready = {0: [], 1: []}
            for h in range(16):
                ih = half * 16 + h
                par = ih % 2
                war_up = list(glue.get(ih - 2, []))
                if h < 2:
                    war_up = war_up + down_war + list(start_deps)
                ld1 = r.dma("pool", lambda e, p_=par, h_=h: e.dma_start(
                    out=SW1s[p_].rearrange("p (kc n) -> p kc n", n=128),
                    in_=sw1_d.rearrange("(kc p) n -> p kc n", p=128)[:, :, h_ * 128:(h_ + 1) * 128]),
                    deps=[up_pe.get(ih - 2)] + list(dma_deps), chan="sw1p%d" % par)
                ld3 = r.dma("pool", lambda e, p_=par, h_=h: e.dma_start(
                    out=SW3s[p_].rearrange("p (kc n) -> p kc n", n=128),
                    in_=sw3_d.rearrange("(kc p) n -> p kc n", p=128)[:, :, h_ * 128:(h_ + 1) * 128]),
                    deps=[up_pe.get(ih - 2)] + list(dma_deps), chan="sw3p%d" % par)

                def up(e, p_=par, hofs_=hofs):
                    for g2 in range(2):
                        for kc in range(NCH):
                            e.matmul(UB[p_][2 * g2], SW1s[p_][:, kc * 128:(kc + 1) * 128],
                                     XH[:, kc * T + hofs_ + g2 * 512: kc * T + hofs_ + g2 * 512 + 512],
                                     start=(kc == 0), stop=(kc == NCH - 1))
                        for kc in range(NCH):
                            i = e.matmul(UB[p_][2 * g2 + 1], SW3s[p_][:, kc * 128:(kc + 1) * 128],
                                         XH[:, kc * T + hofs_ + g2 * 512: kc * T + hofs_ + g2 * 512 + 512],
                                         start=(kc == 0), stop=(kc == NCH - 1))
                    return i
                if ih == 1:
                    rec_sw2()
                o_up = r.pe(up, deps=[ld1, ld3] + xh_dep + war_up)
                up_pe[ih] = o_up

                gl = []
                for g2 in range(2):
                    t1war = [g_last.get((par, 1 - g2))] if g2 == 0 else [gl[-1]]
                    o_t1 = r.act(lambda e, p_=par, g2_=g2, h_=h: e.activation(
                        T1b[p_][:], UB[p_][2 * g2_], AF.Silu, bias=sb1c[:, h_:h_ + 1]),
                        deps=[o_up] + t1war)
                    o_t3 = r.dve(lambda e, p_=par, g2_=g2, h_=h: e.tensor_scalar_add(
                        T3b[p_][:], UB[p_][2 * g2_ + 1], sb3c[:, h_:h_ + 1]),
                        deps=[o_up] + t1war)
                    o_g = r.dve(lambda e, p_=par, g2_=g2, h_=h: e.tensor_mul(
                        Gh[:, h_ * 1024 + g2_ * 512: h_ * 1024 + g2_ * 512 + 512],
                        T1b[p_][:], T3b[p_][:]), deps=[o_t1, o_t3])
                    gl += [o_t1, o_t3, o_g]
                    g_last[(par, g2)] = o_g
                    g_ready[g2].append(o_g)
                glue[ih] = gl

            prev_comb = {}
            last_upg = [up_pe[half * 16 + 15]] + glue[half * 16 + 15] + glue[half * 16 + 14]
            for g2 in range(2):
                for c in range(NCH):
                    yb = [PS_A[:, 0:512], PS_B[:, 0:512]][c % 2]

                    def down(e, g2_=g2, c_=c, yb_=yb):
                        for hc in range(16):
                            e.matmul(yb_, SW2[:, hc * 1024 + c_ * 128: hc * 1024 + c_ * 128 + 128],
                                     Gh[:, hc * 1024 + g2_ * 512: hc * 1024 + g2_ * 512 + 512],
                                     start=(hc == 0), stop=False)
                        return e.matmul(yb_, sb2r[0:1, c_ * 128:(c_ + 1) * 128],
                                        onesr[0:1, 0:512], start=False, stop=True)
                    o_y = r.pe(down, deps=ld_sw2_holder + g_ready[g2] + last_upg +
                               ([prev_comb[c % 2]] if c % 2 in prev_comb else []))
                    o_cb = r.dve(lambda e, g2_=g2, c_=c, yb_=yb, hofs_=hofs: e.scalar_tensor_tensor(
                        X[:, c_ * T + hofs_ + g2_ * 512: c_ * T + hofs_ + g2_ * 512 + 512],
                        yb_, 0.25,
                        X[:, c_ * T + hofs_ + g2_ * 512: c_ * T + hofs_ + g2_ * 512 + 512],
                        ALU.mult, ALU.add), deps=[o_y] + list(start_deps))
                    prev_comb[c % 2] = o_cb
                    down_done.append(o_cb)
            down_war = [prev_comb.get(0), prev_comb.get(1)]

        last_down = [o_y]
        pack_ops = []
        for c in range(NCH):
            for gh in range(2):
                base = c * T + gh * 1024

                def packf(e, base_=base):
                    return _copy(e,
                        XP[:, base_: base_ + 1024].rearrange("p (r o) -> p r o", r=8),
                        XH[:, base_: base_ + 1024].rearrange("p (o i) -> p o i", i=8).transpose([0, 2, 1]))
                pack_ops.append(r.op("dve" if (c + gh) % 2 == 0 else "act", packf,
                                     deps=last_down +
                                     xh_groups[gh * 2] + xh_groups[gh * 2 + 1]))

        w1q_rd = {0: list(down_war), 1: list(down_war)}
        hb_rd = {j: list(down_war) for j in range(4)}
        yb_rd = {0: down_done[-1:], 1: down_done[-1:]}
        ldq = {}

        w1q_st = {0: None, 1: None}
        w2_st = {0: None, 1: None}

        def load_w1q(qg):
            e_, q = qg // 4, qg % 4
            deps = list(w1q_rd[qg % 2]) if qg >= 2 else list(dma_deps)
            if scratch == "save" and w1q_st[qg % 2] is not None:
                deps = deps + [w1q_st[qg % 2]]
            parts = []
            if scratch == "load":
                for h2 in range(2):
                    parts.append(r.dma(("sync", "act")[h2], lambda e, eq=qg, b_=qg % 2, h2_=h2: e.dma_start(
                        out=W1q[b_][:, h2_ * 1024:(h2_ + 1) * 1024],
                        in_=sw1s_d[eq, :, h2_ * 1024:(h2_ + 1) * 1024]),
                        deps=deps, chan="w1qL%d%s" % (qg % 2, "ab"[h2])))
            else:
                for h2 in range(2):
                    parts.append(r.dma("pool", lambda e, e2=e_, q_=q, b_=qg % 2, h2_=h2: e.dma_start(
                        out=W1q[b_].rearrange("p (kc n) -> p kc n", n=256)[:, h2_ * 4:(h2_ + 1) * 4, :],
                        in_=w1_d[e2].rearrange("(kc p) n -> p kc n", p=128)[:, h2_ * 4:(h2_ + 1) * 4, q_ * 256:(q_ + 1) * 256]),
                        deps=deps, chan="w1q%d%s" % (qg % 2, "ab"[h2])))
            ldq[qg] = parts
            if scratch == "save":
                w1q_st[qg % 2] = r.dma("sync", lambda e, eq=qg, b_=qg % 2: e.dma_start(
                    out=sw1s_d[eq], in_=W1q[b_][:, :]), deps=parts, chan="stw1")
                scr_sts["w1"] = w1q_st[qg % 2]

        load_w1q(0)
        load_w1q(1)
        ld_w2 = {}
        ld_b2 = {}
        last_md = {}

        def load_w2(e_):
            dep = [last_md.get(e_ - 2)] if e_ >= 2 else list(down_war) + list(ldq[0])
            if scratch == "save" and w2_st[e_ % 2] is not None:
                dep = dep + [w2_st[e_ % 2]]
            ld_b2[e_] = r.dma("pool", lambda e, e2=e_, b_=e_ % 2: e.dma_start(
                out=b2s[b_], in_=b2r_d[0:1, e2 * D:(e2 + 1) * D]), deps=dep + down_done[-1:],
                chan="b2_%d" % (e_ % 2))
            parts = []
            if scratch == "load":
                for h2 in range(2):
                    parts.append(r.dma(("pool", "sync")[h2],
                        lambda e, e2=e_, b_=e_ % 2, h2_=h2: e.dma_start(
                        out=W2s[b_][:, h2_ * 4096:(h2_ + 1) * 4096],
                        in_=sw2s_d[e2, :, h2_ * 4096:(h2_ + 1) * 4096]),
                        deps=dep, chan="w2L%d%s" % (e_ % 2, "ps"[h2])))
            else:
                for h2 in range(4):
                    parts.append(r.dma("pool",
                        lambda e, e2=e_, b_=e_ % 2, h2_=h2: e.dma_start(
                        out=W2s[b_].rearrange("p (kc n) -> p kc n", n=D)[:, h2_ * 2:(h2_ + 1) * 2, :],
                        in_=w2_d[e2].rearrange("(kc p) n -> p kc n", p=128)[:, h2_ * 2:(h2_ + 1) * 2, :]),
                        deps=dep, chan="w2_%d%s" % (e_ % 2, "ps"[h2 % 2])))
            ld_w2[e_] = parts
            if scratch == "save":
                w2_st[e_ % 2] = r.dma("sync", lambda e, e2=e_, b_=e_ % 2: e.dma_start(
                    out=sw2s_d[e2], in_=W2s[b_][:, :]), deps=parts, chan="stw2")
                scr_sts["w2"] = w2_st[e_ % 2]

        load_w2(0)
        load_w2(1)

        scat_all = []
        for e_ in range(E):
            par = e_ % 2
            a_, d_ = RES[e_]
            gels = []
            o_mu = None
            for hr in range(8):
                qg = e_ * 4 + hr // 2
                slot = hr % 4

                def moe_up_f(e, b_=qg % 2, hr_=hr, a2=a_, d2=d_, slot_=slot):
                    hb = HB4[slot_]
                    for kc in range(NCH):
                        i = e.matmul(hb, W1q[b_][:, kc * 256 + (hr_ % 2) * 128: kc * 256 + (hr_ % 2) * 128 + 128],
                                     xp_rhs(kc, a2, d2),
                                     start=(kc == 0), stop=(kc == NCH - 1))
                    return i
                o_mu = r.pe(moe_up_f, deps=list(ldq[qg]) + pack_ops + hb_rd[slot] + down_done[-1:])
                o_gl = r.act(lambda e, hr_=hr, e2=e_, slot_=slot: e.activation(
                    Hs2[:, hr_ * 512:(hr_ + 1) * 512], HB4[slot_], AF.Gelu,
                    bias=b1c[:, e2 * 8 + hr_: e2 * 8 + hr_ + 1]), deps=[o_mu])
                hb_rd[slot] = [o_gl]
                gels.append(o_gl)
                all_gel.append(o_gl)
                if hr % 2 == 1:
                    w1q_rd[qg % 2] = [o_mu]
                    if qg + 2 < E * 4:
                        load_w1q(qg + 2)

            o_md = None
            for c in range(NCH):
                yb = YB2[c % 2]

                def moe_dn(e, e2=e_, c_=c, b_=par, yb_=yb):
                    for hc in range(8):
                        e.matmul(yb_, W2s[b_][:, hc * D + c_ * 128: hc * D + c_ * 128 + 128],
                                 Hs2[:, hc * 512:(hc + 1) * 512],
                                 start=(hc == 0), stop=False)
                    return e.matmul(yb_, b2s[b_][0:1, c_ * 128:(c_ + 1) * 128],
                                    onesr[0:1, 0:512], start=False, stop=True)
                o_md = r.pe(moe_dn, deps=list(ld_w2[e_]) + [ld_b2[e_]] + gels + yb_rd[c % 2])

                def scat(e, c_=c, yb_=yb, a2=a_, d2=d_):
                    for gh in range(2):
                        xv = x_scat(c_, gh, a2, d2)
                        yv = yb_[:, gh * 256:(gh + 1) * 256].rearrange(
                            "p (r o) -> p r o", r=2)
                        i = e.scalar_tensor_tensor(xv, yv, 0.125, xv, ALU.mult, ALU.add)
                    return i
                o_sc = r.dve(scat, deps=[o_md] + down_done[-1:])
                yb_rd[c % 2] = [o_sc]
                scat_all.append(o_sc)
            last_md[e_] = o_md
            if e_ + 2 < E:
                load_w2(e_ + 2)

        tails = [last_md[E - 1], scat_all[-1], scat_all[-2], all_gel[-1],
                 down_done[-1], up_pe[31]]
        if scratch == "save":
            tails = tails + [w1q_st[0], w1q_st[1], w2_st[0], w2_st[1]]
        return tails

    def emit_attn(xh_groups, start_deps):
        xh_all = [op for g_ in xh_groups for op in g_]
        inner_tails = {0: [], 1: []}
        innerA_rd = []
        innerC_rd = []
        innerD_rd = []
        last_xadd = []
        final_exps = []
        final_misc = []

        p_war = {0: [], 1: [], 2: []}
        ptsb_war = {0: [], 1: []}
        ctx_war = {0: [], 1: []}
        rs_war = {0: [], 1: []}
        ptev = {0: None, 1: None}
        sq_warA = []
        sq_warB = []
        trslot_war = []

        for p in range(8):
            par = p % 2
            K2v, Q2v, V2v = K2[par], Q2[par], V2[par]
            AOv = AO2[par]
            pair_dep = list(start_deps) + attn_wready + xh_all + inner_tails[par]

            proj_done = []
            v_done = []
            prev_ln = None
            prev_pe3 = None
            prev_kv_rd = []
            prev_qc_rd = []
            prev_scrd = []
            prev_qr = None
            prev_vev = None
            prev_kr = None
            for g in range(4):
                go = g * 512

                def pe1(e, p_=p, go_=go):
                    xh0 = XH[0:64, p_ * T + go_: p_ * T + go_ + 512]
                    xh1 = XH[64:128, p_ * T + go_: p_ * T + go_ + 512]
                    e.matmul(PS_A[:, 0:512], wkvt[0:64, :], xh0, start=True, stop=True)
                    e.matmul(PS_A[:, 512:1024], wkvt[64:128, :], xh1, start=True, stop=True)
                    e.matmul(PS_B[0:64, 0:512], wqt[0:64, :], xh0, start=True, stop=True)
                    e.matmul(PS_B[0:64, 512:1024], wqt[64:128, :], xh1, start=True, stop=True)
                    e.matmul(PS_C[0:64, 0:512], wkrt[0:64, :], xh0, start=True, stop=True)
                    return e.matmul(PS_C[0:64, 512:1024], wkrt[64:128, :], xh1, start=True, stop=True)
                war_abc = prev_scrd + ([prev_qr, prev_vev, prev_kr] if prev_qr else [])
                o_pe1 = r.pe(pe1, deps=pair_dep + war_abc + innerA_rd + innerC_rd)

                o_kv = r.dve(lambda e: e.tensor_copy(kvsb[:], PS_A[:, 0:1024]),
                             deps=[o_pe1] + prev_kv_rd)
                o_qc = r.dve(lambda e: e.tensor_copy(qcsb[:], PS_B[0:64, 0:1024]),
                             deps=[o_pe1] + prev_qc_rd)
                o_kr = r.act(lambda e, K2v_=K2v, go_=go: e.copy(
                    K2v_[64:128, :].rearrange("p (hh t) -> p hh t", hh=2)[:, :, go_:go_ + 512],
                    PS_C[0:64, 0:1024]), deps=[o_pe1])
                prev_kr = o_kr

                o_s1 = r.act(lambda e: e.activation(sqk[0:64, :], PS_A[0:64, 0:512],
                                                    AF.Square), deps=[o_pe1])
                o_s2 = r.act(lambda e: e.activation(sqk[64:128, :], PS_A[0:64, 512:1024],
                                                    AF.Square), deps=[o_pe1])
                o_s3 = r.act(lambda e: e.activation(sqq[0:64, :], PS_B[0:64, 0:512],
                                                    AF.Square), deps=[o_pe1])
                o_s4 = r.act(lambda e: e.activation(sqq[64:128, :], PS_B[0:64, 512:1024],
                                                    AF.Square), deps=[o_pe1])

                def pe2(e):
                    e.matmul(PS_D[0:2, 0:512], b2ones[:], sqk[:], start=True, stop=True)
                    return e.matmul(PS_D[0:2, 512:1024], b2ones[:], sqq[:], start=True, stop=True)
                o_pe2 = r.pe(pe2, deps=[o_s1, o_s2, o_s3, o_s4, prev_ln] + innerD_rd)

                o_ln2 = r.act(lambda e: e.activation(row2[:], PS_D[0:2, 0:1024], AF.Ln,
                                                     scale=1.0 / 64, bias=epsr2[:]), deps=[o_pe2])
                prev_ln = o_ln2
                o_ex2 = r.act(lambda e: e.activation(rstd2[:], row2[:], AF.Exp, scale=-0.5),
                              deps=[o_ln2, prev_pe3])

                def pe3(e):
                    e.matmul(PS_A[:, 0:512], knw2b[:], rstd2[:, 0:512], start=True, stop=True)
                    return e.matmul(PS_A[:, 512:1024], qnw2b[:], rstd2[:, 512:1024],
                                    start=True, stop=True)
                o_pe3 = r.pe(pe3, deps=[o_ex2, o_kv, o_s1, o_s2])
                prev_pe3 = o_pe3

                o_k0 = r.dve(lambda e, K2v_=K2v, go_=go: e.tensor_mul(
                    K2v_[0:64, go_:go_ + 512], kvsb[0:64, 0:512], PS_A[0:64, 0:512]),
                    deps=[o_pe3])
                o_k1 = r.dve(lambda e, K2v_=K2v, go_=go: e.tensor_mul(
                    K2v_[0:64, T + go_: T + go_ + 512], kvsb[0:64, 512:1024], PS_A[64:128, 0:512]),
                    deps=[o_pe3])
                o_q0 = r.dve(lambda e, Q2v_=Q2v, go_=go: e.tensor_mul(
                    Q2v_[0:64, go_:go_ + 512], qcsb[0:64, 0:512], PS_A[0:64, 512:1024]),
                    deps=[o_pe3])
                o_q1 = r.dve(lambda e, Q2v_=Q2v, go_=go: e.tensor_mul(
                    Q2v_[0:64, T + go_: T + go_ + 512], qcsb[0:64, 512:1024], PS_A[64:128, 512:1024]),
                    deps=[o_pe3])
                prev_scrd = [o_k0, o_k1, o_q0, o_q1]

                def pe4(e, Q2v_=Q2v, go_=go):
                    e.matmul(PS_B[0:64, 0:512], wqrt[:], Q2v_[0:64, go_:go_ + 512],
                             start=True, stop=True)
                    e.matmul(PS_B[0:64, 512:1024], wqrt[:], Q2v_[0:64, T + go_: T + go_ + 512],
                             start=True, stop=True)
                    for hh in range(2):
                        for tl in range(4):
                            i = e.matmul(PS_C[:, (hh * 4 + tl) * 64:(hh * 4 + tl) * 64 + 64],
                                         kvsb[64:128, hh * 512 + tl * 128: hh * 512 + tl * 128 + 128],
                                         wot[64:128, :], start=True, stop=True)
                    return i
                o_pe4 = r.pe(pe4, deps=[o_q0, o_q1, o_qc, o_kr, o_kv, o_s3, o_s4])

                o_qr = r.act(lambda e, Q2v_=Q2v, go_=go: e.copy(
                    Q2v_[64:128, :].rearrange("p (hh t) -> p hh t", hh=2)[:, :, go_:go_ + 512],
                    PS_B[0:64, 0:1024]), deps=[o_pe4])
                prev_qr = o_qr
                o_vev = r.dve(lambda e, V2v_=V2v, g_=g: e.tensor_copy(
                    V2v_.rearrange("p (hh tt o) -> p hh tt o", hh=2, tt=16)[:, :, g_ * 4:(g_ + 1) * 4, 0:64],
                    PS_C[:, 0:512].rearrange("p (hh tl o) -> p hh tl o", hh=2, tl=4)),
                    deps=[o_pe4])
                prev_vev = o_vev
                prev_kv_rd = [o_pe4, o_k0, o_k1, o_s1, o_s2]
                prev_qc_rd = [o_q0, o_q1, o_s3, o_s4]
                proj_done += [o_k0, o_k1, o_q0, o_q1, o_qr, o_kr]
                v_done.append(o_vev)

            projC_rd = prev_vev
            projD_rd = prev_ln

            bias_ops = {}
            for hh in range(2):
                nb = 584 + hh * 32
                o_sq = r.dve(lambda e, hh_=hh, Q_=Q2v: e.tensor_mul(
                    SQA, Q_[:, hh_ * T:(hh_ + 1) * T], Q_[:, hh_ * T:(hh_ + 1) * T]),
                    deps=proj_done + sq_warA)
                o_sk = r.dve(lambda e, hh_=hh, K_=K2v: e.tensor_mul(
                    SQB, K_[:, hh_ * T:(hh_ + 1) * T], K_[:, hh_ * T:(hh_ + 1) * T]),
                    deps=proj_done + sq_warB)

                def normq(e, nb_=nb):
                    for t in range(16):
                        i = e.matmul(PS_D[:, nb_ + t: nb_ + t + 1],
                                     SQA[:, t * 128:(t + 1) * 128], onesc[:],
                                     start=True, stop=True)
                    return i
                o_nq = r.pe(normq, deps=[o_sq, prev_ln] + ctx_war[1])

                def normk(e, nb_=nb):
                    for t in range(16):
                        i = e.matmul(PS_D[:, nb_ + 16 + t: nb_ + 17 + t],
                                     SQB[:, t * 128:(t + 1) * 128], onesc[:],
                                     start=True, stop=True)
                    return i
                o_nk = r.pe(normk, deps=[o_sk, prev_ln] + ctx_war[1])
                sq_warA = [o_nq]
                sq_warB = [o_nk]
                o_rq = r.dve(lambda e, hh_=hh, nb_=nb: e.reduce_max(
                    bqk[:, hh_ * 2: hh_ * 2 + 1], PS_D[:, nb_: nb_ + 16], axis=AX.X),
                    deps=[o_nq])
                o_rk = r.dve(lambda e, hh_=hh, nb_=nb: e.reduce_max(
                    bqk[:, hh_ * 2 + 1: hh_ * 2 + 2], PS_D[:, nb_ + 16: nb_ + 32], axis=AX.X),
                    deps=[o_nk])
                o_t = r.pe(lambda e, hh_=hh: e.transpose(
                    PS_D[0:2, 840:968], bqk[:, hh_ * 2: hh_ * 2 + 2], identf[:]),
                    deps=[o_rq, o_rk] + trslot_war)
                o_r2 = r.dve(lambda e: e.reduce_max(
                    bqk[0:2, 4:5], PS_D[0:2, 840:968], axis=AX.X), deps=[o_t])
                o_ln = r.act(lambda e: e.activation(
                    lnc[:], bqk[0:2, 4:5], AF.Ln), deps=[o_r2])
                o_ps = r.pe(lambda e, hh_=hh: e.matmul(
                    PS_D[0:1, 968 + hh_: 969 + hh_], onesc[0:2, :], lnc[:],
                    start=True, stop=True), deps=[o_ln])
                o_ex = r.act(lambda e, hh_=hh: e.activation(
                    mbf[0:1, hh_:hh_ + 1], PS_D[0:1, 968 + hh_: 969 + hh_],
                    AF.Exp, scale=0.5), deps=[o_ps])
                trslot_war = [o_r2, o_ps]
                o_bb = r.pe(lambda e, hh_=hh: e.matmul(
                    PS_D[:, 972 + hh_: 973 + hh_], negon[:], mbf[0:1, hh_:hh_ + 1],
                    start=True, stop=True), deps=[o_ex])
                o_bf = r.act(lambda e, hh_=hh, bc_=par * 2 + hh: e.copy(
                    mxc[:, bc_:bc_ + 1], PS_D[:, 972 + hh_: 973 + hh_]), deps=[o_bb])
                bias_ops[hh] = o_bf

            def vones(e, V2v_=V2v):
                e.memset(V2v_.rearrange("p (tt o) -> p tt o", o=68)[:, :, 64:65], 1.0)
                return e.memset(V2v_.rearrange("p (tt o) -> p tt o", o=68)[:, :, 65:68], 0.0)
            o_ones = r.pool(vones, deps=pair_dep)

            RPAR = [
                dict(S=PS_A, PTps=PS_C[:, 0:512].bitcast(BF16), PTsb=PTs[:, 0:1024]),
                dict(S=PS_B, PTps=PS_D[:, 0:512].bitcast(BF16), PTsb=PTs[:, 1024:2048]),
            ]
            PSLOT = [Psb[:, 0:1024], Psb[:, 1024:2048], P3]
            CTXs = [PS_C[:, 512:580], PS_D[:, 512:580]]
            s_war = {0: list(proj_done), 1: list(proj_done)}
            resc_last = [None]
            pv_ref = {}
            soft_ref = {}
            evs_ref = {}

            tasks0 = []
            for qi in range(16):
                kl = 128 * (qi + 1)
                nch = (kl + 1023) // 1024
                for j in range(nch):
                    tasks0.append((qi, j, min(1024, kl - j * 1024), nch))

            ao_done = []
            for hh in range(2):
                ntk = len(tasks0)
                for m in range(ntk + 4):
                    if 0 <= m - 3 < ntk:
                        mm2 = m - 3
                        qi2, j2, w2, nch2 = tasks0[mm2]
                        par2 = mm2 % 2
                        R2 = RPAR[par2]
                        Pv2 = PSLOT[mm2 % 3]
                        nblkj = w2 // 128
                        exps2 = soft_ref[mm2]

                        def trs(e, R_=R2, P_=Pv2, nblkj_=nblkj):
                            for b in range(nblkj_):
                                i = e.transpose(R_["PTps"][:, b * 128:(b + 1) * 128],
                                                P_[:, b * 128:(b + 1) * 128], identb[:])
                            return i
                        o_tr = r.pe(trs, deps=exps2 +
                                    ([ptev[par2]] if ptev[par2] else []) +
                                    [projC_rd if par2 == 0 else projD_rd])
                        p_war[mm2 % 3] = [o_tr]
                        halves = [(0, min(4, nblkj))] + ([(4, nblkj)] if nblkj > 4 else [])
                        evs = []
                        for (b0, b1) in halves:
                            o_pt = r.op("dve" if par2 == 0 else "act",
                                        lambda e, R_=R2, b0_=b0, b1_=b1: _copy(
                                            e, R_["PTsb"][:, b0_ * 128:b1_ * 128],
                                            R_["PTps"][:, b0_ * 128:b1_ * 128]),
                                        deps=[o_tr] + ptsb_war[par2])
                            evs.append(o_pt)
                        ptev[par2] = evs[-1]
                        evs_ref[mm2] = (halves, evs)
                    if 0 <= m - 4 < ntk:
                        mm3 = m - 4
                        qi3, j3, w3, nch3 = tasks0[mm3]
                        par3 = mm3 % 2
                        R3 = RPAR[par3]
                        nblkj = w3 // 128
                        halves, evs = evs_ref[mm3]
                        cpar = qi3 % 2
                        ctx = CTXs[cpar]
                        pvs_ops = []
                        for hi, (b0, b1) in enumerate(halves):
                            def pvs(e, R_=R3, V2v_=V2v, hh_=hh, j_=j3, b0_=b0, b1_=b1,
                                    nch_=nch3, nblkj_=nblkj, ctx_=ctx):
                                for b in range(b0_, b1_):
                                    kb = j_ * 8 + b
                                    i = e.matmul(ctx_, R_["PTsb"][:, b * 128:(b + 1) * 128],
                                                 V2v_[:, hh_ * 1088 + kb * 68: hh_ * 1088 + kb * 68 + 68],
                                                 start=(j_ == 0 and b == 0),
                                                 stop=(j_ == nch_ - 1 and b == nblkj_ - 1))
                                return i
                            dep0 = ctx_war[cpar] if (j3 == 0 and hi == 0) else []
                            o_pv = r.pe(pvs, deps=[evs[hi], o_ones] + v_done + list(dep0))
                            pvs_ops.append(o_pv)
                        ptsb_war[par3] = pvs_ops
                        pv_ref[mm3] = o_pv
                        if j3 == nch3 - 1:
                            qb = qi3 % 2
                            ao = AOv[:, hh * 1024 + qi3 * 64: hh * 1024 + qi3 * 64 + 64]
                            o_rs = r.dve(lambda e, qb_=qb, ctx_=ctx: e.reciprocal(
                                rs[:, qb_:qb_ + 1], ctx_[:, 64:65]),
                                deps=[o_pv] + rs_war[qb])
                            o_resc = r.dve(lambda e, ao_=ao, qb_=qb, ctx_=ctx: e.tensor_scalar_mul(
                                ao_, ctx_[:, 0:64], rs[:, qb_:qb_ + 1]),
                                deps=[o_rs, o_pv], fence=True)
                            ctx_war[cpar] = [o_resc, o_rs]
                            rs_war[qb] = [o_resc]
                            resc_last[0] = o_resc
                    if m < ntk:
                        qi, j, w, nch = tasks0[m]
                        parm = m % 2
                        R = RPAR[parm]

                        def smm(e, R_=R, K2v_=K2v, Q2v_=Q2v, hh_=hh, qi_=qi, j_=j, w_=w):
                            for s in range(0, w_, 512):
                                n = min(512, w_ - s)
                                i = e.matmul(R_["S"][:, s:s + n],
                                             Q2v_[:, hh_ * T + qi_ * 128: hh_ * T + qi_ * 128 + 128],
                                             K2v_[:, hh_ * T + j_ * 1024 + s: hh_ * T + j_ * 1024 + s + n],
                                             start=True, stop=True)
                            return i
                        Pv = PSLOT[m % 3]
                        o_s = r.pe(smm, deps=s_war[parm])
                        o_e = r.op("act", lambda e, R_=R, P_=Pv, w_=w, bc_=par * 2 + hh: e.activation(
                            P_[:, 0:w_], R_["S"][:, 0:w_], AF.Exp,
                            bias=mxc[:, bc_:bc_ + 1]),
                            deps=[o_s, bias_ops[hh]] + p_war[m % 3], fence=True)
                        exp_ops = [o_e]
                        if j == nch - 1:
                            o_msk = r.pool(lambda e, P_=Pv, w_=w: e.affine_select(
                                P_[:, w_ - 128: w_], P_[:, w_ - 128: w_],
                                pattern=[[-1, 128]], compare_op=ALU.is_ge, fill=0.0,
                                base=0, channel_multiplier=1), deps=[o_e])
                            exp_ops.append(o_msk)
                        s_war[parm] = [o_e]
                        soft_ref[m] = exp_ops
                ao_done.append(resc_last[0])
            slot_rd = {0: [pv_ref[len(tasks0) - 1]], 1: [pv_ref[len(tasks0) - 1]]}
            slotT = [PS_C[0:64, 768:832].bitcast(BF16), PS_D[0:64, 768:832].bitcast(BF16)]
            xa_ops = []
            for k in range(32):
                hh_f, qi_f = k % 2, k // 2
                sl = k % 2
                ctxT = slotT[sl]

                def btr(e, hh_=hh_f, qi_=qi_f, ctxT_=ctxT, AOv_=AOv):
                    return e.transpose(ctxT_,
                                       AOv_[:, hh_ * 1024 + qi_ * 64: hh_ * 1024 + qi_ * 64 + 64],
                                       identb[:])
                o_btr = r.pe(btr, deps=ao_done + [x for x in slot_rd[sl] if x])
                o_xa = r.dve(lambda e, hh_=hh_f, qi_=qi_f, p_=p, ctxT_=ctxT: e.tensor_add(
                    X[64 * hh_: 64 * hh_ + 64, p_ * T + qi_ * 128: p_ * T + qi_ * 128 + 128],
                    ctxT_,
                    X[64 * hh_: 64 * hh_ + 64, p_ * T + qi_ * 128: p_ * T + qi_ * 128 + 128]),
                    deps=[o_btr])
                slot_rd[sl] = [o_xa]
                xa_ops.append(o_xa)
            innerA_rd = s_war[0] + s_war[1]
            innerC_rd = [ptev[0]] + ctx_war[0] + xa_ops[-2:]
            innerD_rd = [ptev[1]] + ctx_war[1] + xa_ops[-2:]
            last_xadd += xa_ops[-2:]
            final_exps = s_war[0] + s_war[1]
            final_misc = [ptev[0], ptev[1], resc_last[0]]
            inner_tails[par] = xa_ops[-4:] + final_exps + final_misc
        tails = last_xadd[-4:] + final_exps[-2:] + final_misc
        return [t for t in tails if t is not None]

    tails = list(ld_x) + consts
    do_scr = False
    if 1 in stages:
        g1, t1 = emit_ln(list(ld_x) + consts)
        tails = emit_ffn(g1, t1, consts, scratch="save" if do_scr else None)
    if 2 in stages:
        g2ln, t2 = emit_ln(tails)
        tails = emit_attn(g2ln, t2)
    if 3 in stages:
        g3ln, t3 = emit_ln(tails)
        extra = [scr_sts.get("w1"), scr_sts.get("w2")] if do_scr else []
        tails = emit_ffn(g3ln, t3 + [s for s in extra if s], t3,
                         scratch="load" if do_scr else None)

    st_ops = []
    for h2 in range(2):
        st_ops.append(r.dma(("sync", "act")[h2], lambda e, h2_=h2: e.dma_start(
            out=out_d.rearrange("(c p) t -> p c t", p=128)[:, h2_ * 4:(h2_ + 1) * 4, :],
            in_=X.rearrange("p (c t) -> p c t", t=T)[:, h2_ * 4:(h2_ + 1) * 4, :]),
            deps=tails, chan="xo%d" % h2))
    r.op("sync", lambda e: e.nop(), deps=st_ops)

    with nc.Block() as blk:
        r.emit(blk)
    return nc, es


_CACHE = {}


def _prep(inputs):
    _ = inputs.get("L0_ffn_t_emb")
    f32 = np.float32
    bf = ml_dtypes.bfloat16
    d = {}
    d["sw1"] = np.ascontiguousarray(np.asarray(inputs["L0_ffn_sw1"], f32))
    d["sw3"] = np.ascontiguousarray(np.asarray(inputs["L0_ffn_sw3"], f32))
    d["sw2"] = np.ascontiguousarray(np.asarray(inputs["L0_ffn_sw2"], f32))
    d["w1"] = np.ascontiguousarray(np.asarray(inputs["L0_ffn_w1"], f32))
    d["w2"] = np.ascontiguousarray(np.asarray(inputs["L0_ffn_w2"], f32))
    d["sb1c"] = np.ascontiguousarray(np.asarray(inputs["L0_ffn_sb1"], f32).reshape(16, 128).T)
    d["sb3c"] = np.ascontiguousarray(np.asarray(inputs["L0_ffn_sb3"], f32).reshape(16, 128).T)
    d["sb2r"] = np.asarray(inputs["L0_ffn_sb2"], f32).reshape(1, D)
    d["b1c"] = np.ascontiguousarray(
        np.asarray(inputs["L0_ffn_b1"], f32).reshape(E, 8, 128).transpose(2, 0, 1).reshape(128, 64))
    d["b2r"] = np.asarray(inputs["L0_ffn_b2"], f32).reshape(1, E * D)
    d["wdq"] = np.ascontiguousarray(np.asarray(inputs["L0_attn_W_DQ"], f32))
    d["wuqt"] = np.ascontiguousarray(np.asarray(inputs["L0_attn_W_UQ"], f32).T)
    d["wdkv"] = np.ascontiguousarray(np.asarray(inputs["L0_attn_W_DKV"], f32))
    d["wukvt"] = np.ascontiguousarray(np.asarray(inputs["L0_attn_W_UKV"], f32).T)
    d["wqrt"] = np.ascontiguousarray(np.asarray(inputs["L0_attn_W_QR"], f32).T)
    wkrt_t = np.asarray(inputs["L0_attn_W_KR"], f32).T
    d["wkrt"] = np.ascontiguousarray(np.concatenate([wkrt_t, wkrt_t], axis=0))
    wot_t = np.asarray(inputs["L0_attn_W_O"], f32).T
    d["wot"] = np.ascontiguousarray(np.concatenate([wot_t, wot_t], axis=0))
    knw2 = np.zeros((2, 128), f32)
    knw2[0, 0:64] = np.asarray(inputs["L0_attn_k_norm_w"], f32)
    knw2[1, 64:128] = np.asarray(inputs["L0_attn_k_norm_w"], f32)
    d["knw2"] = knw2
    qnw2 = np.zeros((2, 128), f32)
    qnw2[0, 0:64] = np.asarray(inputs["L0_attn_q_norm_w"], f32)
    qnw2[1, 64:128] = np.asarray(inputs["L0_attn_q_norm_w"], f32)
    d["qnw2"] = qnw2
    d["identb"] = np.eye(128, dtype=bf)
    d["identf"] = np.eye(128, dtype=f32)
    b2o = np.zeros((128, 2), bf)
    b2o[0:64, 0] = 1
    b2o[64:128, 1] = 1
    d["b2ones"] = b2o
    d["onesr"] = np.ones((33, 512), bf)
    d["onesc"] = np.ones((128, 1), bf)
    d["epsln"] = np.full((1, 1), EPS_LN, f32)
    d["epsr2"] = np.full((2, 1), EPS_RMS, f32)
    return d


def kernel(**inputs):
    key = "full"
    if key not in _CACHE:
        _CACHE[key] = build_program()
    nc, _ = _CACHE[key]
    shared = _prep(inputs)
    x = np.asarray(inputs["L0_attn_x"], np.float32)
    in_maps = []
    for i in range(8):
        m = dict(shared)
        m["x"] = np.ascontiguousarray(x[i].T)
        in_maps.append(m)
    res = run_bass_kernel_spmd(nc, in_maps, core_ids=list(range(8)))
    out = np.stack([np.ascontiguousarray(rr["out"].T) for rr in res.results])
    return out.astype(np.float32)


# version 26 (final)
# speedup vs baseline: 1.0340x; 1.0340x over previous
import numpy as np
import ml_dtypes
from contextlib import ExitStack

import concourse.bass as bass
import concourse.mybir as mybir
from concourse.bass_utils import run_bass_kernel_spmd

F32 = mybir.dt.float32
BF16 = mybir.dt.bfloat16
AF = mybir.ActivationFunctionType
ALU = mybir.AluOpType
AX = mybir.AxisListType

D = 1024
T = 2048
NCH = 8
HS = 2048
HR = 1024
E = 8
EPS_LN = 1e-5
EPS_RMS = 1.1920929e-07


class Op:
    __slots__ = ("eng", "fn", "deps", "dma", "chan", "fence", "needed", "sem", "tick")

    def __init__(self, eng, fn, deps, dma, chan=None, fence=False):
        self.eng, self.fn, self.dma, self.chan = eng, fn, dma, chan
        self.fence = fence
        self.deps = [d for d in deps if d is not None]
        self.needed = False
        self.sem = None
        self.tick = None


class Rec:

    ENGINES = ("pe", "act", "dve", "pool", "sync")

    def __init__(self, nc, es):
        self.nc = nc
        self.es = es
        self.ops = []
        self.sems = {e: es.enter_context(nc.semaphore("s_" + e)) for e in self.ENGINES}
        self.dsems = {}

    def op(self, eng, fn, deps=(), dma=False, chan=None, fence=False):
        o = Op(eng, fn, deps, dma, chan, fence)
        self.ops.append(o)
        return o

    def pe(self, fn, deps=()):
        return self.op("pe", fn, deps)

    def act(self, fn, deps=()):
        return self.op("act", fn, deps)

    def dve(self, fn, deps=(), fence=False):
        return self.op("dve", fn, deps, fence=fence)

    def pool(self, fn, deps=()):
        return self.op("pool", fn, deps)

    def dma(self, queue, fn, deps=(), chan="d0"):
        if chan not in self.dsems:
            self.dsems[chan] = self.es.enter_context(self.nc.semaphore("dm_" + chan))
        return self.op(queue, fn, deps, dma=True, chan=chan)

    def emit(self, blk):
        for o in self.ops:
            if o.dma:
                o.needed = True
            for d in o.deps:
                if d.eng == o.eng and not d.dma and not o.fence:
                    continue
                d.needed = True
        cnt = {}
        for o in self.ops:
            if not o.needed:
                continue
            if o.dma:
                o.sem = self.dsems[o.chan]
                key = ("d", o.chan)
                cnt[key] = cnt.get(key, 0) + 16
            else:
                o.sem = self.sems[o.eng]
                key = ("c", o.eng)
                cnt[key] = cnt.get(key, 0) + 1
            o.tick = cnt[key]

        per_eng = {e: [o for o in self.ops if o.eng == e] for e in self.ENGINES}

        def body(name):
            def run(eng):
                waited = {}
                for o in per_eng[name]:
                    for d in o.deps:
                        if d.eng == o.eng and not d.dma and not o.fence:
                            continue
                        if waited.get(d.sem, 0) < d.tick:
                            eng.wait_ge(d.sem, d.tick)
                            waited[d.sem] = d.tick
                    instr = o.fn(eng)
                    if o.needed:
                        instr.then_inc(o.sem, 16 if o.dma else 1)
            return run

        blk.tensor(body("pe"))
        blk.scalar(body("act"))
        blk.vector(body("dve"))
        blk.gpsimd(body("pool"))
        blk.sync(body("sync"))


def _copy(eng, out, in_):
    if eng.engine == mybir.EngineType.Activation:
        return eng.copy(out, in_)
    return eng.tensor_copy(out, in_)


def build_program(stages=(1, 2, 3)):
    nc = bass.Bass("TRN2", target_bir_lowering=False)
    es = ExitStack()

    def din(name, shape, dtype=F32):
        return nc.dram_tensor(name, list(shape), dtype, kind="ExternalInput").ap()

    x_d = din("x", (D, T))
    sw1_d = din("sw1", (D, HS))
    sw3_d = din("sw3", (D, HS))
    sw2_d = din("sw2", (HS, D))
    w1_d = din("w1", (E, D, HR))
    w2_d = din("w2", (E, HR, D))
    sb1c_d = din("sb1c", (128, 16))
    sb3c_d = din("sb3c", (128, 16))
    sb2r_d = din("sb2r", (1, D))
    b1c_d = din("b1c", (128, 64))
    b2r_d = din("b2r", (1, E * D))
    wdq_d = din("wdq", (512, 64))
    wuqt_d = din("wuqt", (512, 64))
    wdkv_d = din("wdkv", (1024, 64))
    wukvt_d = din("wukvt", (1024, 128))
    wqrt_d = din("wqrt", (64, 64))
    wkrt_d = din("wkrt", (128, 64))
    wot_d = din("wot", (128, 64))
    knw2_d = din("knw2", (2, 128))
    qnw2_d = din("qnw2", (2, 128))
    identb_d = din("identb", (128, 128), BF16)
    identf_d = din("identf", (128, 128), F32)
    b2ones_d = din("b2ones", (128, 2), BF16)
    onesr_d = din("onesr", (33, 512), BF16)
    onesc_d = din("onesc", (128, 1), BF16)
    epsln_d = din("epsln", (1, 1), F32)
    epsr2_d = din("epsr2", (2, 1), F32)
    out_d = nc.dram_tensor("out", [D, T], F32, kind="ExternalOutput").ap()
    sw1s_d = nc.dram_tensor("s_w1", [32, 128, 2048], BF16, kind="Internal").ap()
    sw2s_d = nc.dram_tensor("s_w2", [8, 128, 8192], BF16, kind="Internal").ap()

    def sbt(name, shape, dtype):
        return es.enter_context(nc.sbuf_tensor("sb_" + name, list(shape), dtype))

    X = sbt("X", (128, NCH * T), F32)
    XH = sbt("XH", (128, NCH * T), BF16)

    PK1 = sbt("PK1", (65, 1024), F32)
    PK2 = sbt("PK2", (33, 1024), F32)
    PKB = sbt("PKB", (33, 1024), BF16)
    m_row = PK1[0:1, :]
    ms2_row = PK1[32:33, :]
    b_row = PK1[64:65, :]
    a_row = PK2[0:1, :]
    var_row = PK2[32:33, :]
    abf_row = PKB[0:1, :]
    bbf_row = PKB[32:33, :]

    sb1c = sbt("sb1c", (128, 16), F32)
    sb3c = sbt("sb3c", (128, 16), F32)
    b1c = sbt("b1c", (128, 64), F32)
    sb2r = sbt("sb2r", (1, D), BF16)
    mxc = sbt("mxc", (128, 8), F32)
    rs = sbt("rs", (128, 8), F32)
    bqk = sbt("bqk", (128, 12), F32)
    mbf = sbt("mbf", (1, 2), BF16)
    lnc = sbt("lnc", (2, 1), BF16)
    negon = sbt("negon", (1, 128), BF16)

    wdq = sbt("wdq", (128, 4 * 64), BF16)
    wuqt = sbt("wuqt", (128, 4 * 64), BF16)
    wdkv = sbt("wdkv", (128, 8 * 64), BF16)
    wukvt = sbt("wukvt", (128, 8 * 128), BF16)
    wqrt = sbt("wqrt", (64, 64), BF16)
    wkrt = sbt("wkrt", (128, 64), BF16)
    wot = sbt("wot", (128, 64), BF16)
    wqt = sbt("wqt", (128, 64), BF16)
    wkvt = sbt("wkvt", (128, 128), BF16)
    knw2 = sbt("knw2", (2, 128), F32)
    qnw2 = sbt("qnw2", (2, 128), F32)
    knw2b = sbt("knw2b", (2, 128), BF16)
    qnw2b = sbt("qnw2b", (2, 128), BF16)
    identb = sbt("identb", (128, 128), BF16)
    identf = sbt("identf", (128, 128), F32)
    b2ones = sbt("b2ones", (128, 2), BF16)
    onesr = sbt("onesr", (33, 512), BF16)
    onesc = sbt("onesc", (128, 1), BF16)
    epsln = sbt("epsln", (1, 1), F32)
    epsr2 = sbt("epsr2", (2, 1), F32)

    AW = 21504
    ARENA = sbt("ARENA", (128, AW), F32)

    def av_bf(lo, hi, parts=128):
        return ARENA[0:parts, lo:hi].bitcast(BF16)

    def av_f(lo, hi, parts=128):
        return ARENA[0:parts, lo:hi]

    SW2 = av_bf(0, 8192)
    Gh = av_bf(8192, 16384)
    SW1s = [av_bf(16384, 16896), av_bf(16896, 17408)]
    SW3s = [av_bf(17408, 17920), av_bf(17920, 18432)]
    T1b = [av_bf(18432, 18688), av_bf(18688, 18944)]
    T3b = [av_bf(18944, 19200), av_bf(19200, 19456)]
    W1q = [av_bf(19456, 20480), av_bf(20480, 21504)]
    XP = av_bf(0, 8192)
    Hs2 = av_bf(8192, 10240)
    b2s = [av_bf(10240, 10752, 1), av_bf(10752, 11264, 1)]
    W2s = [av_bf(11264, 15360), av_bf(15360, 19456)]
    XBF2 = [av_bf(8192, 10240), av_bf(10240, 12288)]
    XSQ2 = [av_bf(12288, 14336), av_bf(14336, 16384)]
    Asb2 = [sbt("Asb0", (128, 512), BF16), sbt("Asb1", (128, 512), BF16)]
    Bsb2 = [sbt("Bsb0", (128, 512), BF16), sbt("Bsb1", (128, 512), BF16)]
    TMUL = sbt("TMUL", (128, 512), BF16)
    K2 = [av_bf(0, 2048), av_bf(2048, 4096)]
    Q2 = [av_bf(4096, 6144), av_bf(6144, 8192)]
    V2 = [av_bf(8192, 9280), av_bf(9280, 10368)]
    Psb = av_bf(10368, 11392)
    PTs = av_bf(11392, 12416)
    kvsb = av_bf(12416, 12928)
    qcsb = av_bf(12928, 13440, 64)
    sqk = av_bf(13440, 13696)
    sqq = av_bf(13696, 13952)
    row2 = av_f(13952, 14976, 2)
    rstd2 = av_bf(14976, 15488, 2)
    P3 = av_bf(15488, 16000)
    SQA = av_bf(16384, 17408)
    SQB = av_bf(17408, 18432)
    AO2 = [av_bf(18432, 19456), av_bf(19456, 20480)]

    PS_A = es.enter_context(nc.psum_tensor("PS_A", [128, 1024], F32))
    PS_B = es.enter_context(nc.psum_tensor("PS_B", [128, 1024], F32))
    PS_C = es.enter_context(nc.psum_tensor("PS_C", [128, 1024], F32))
    PS_D = es.enter_context(nc.psum_tensor("PS_D", [128, 1024], F32))
    PTbf = PS_C[:, 0:512].bitcast(BF16)
    SAB = [PS_A, PS_B]

    r = Rec(nc, es)

    ld_x = []
    for h2 in range(2):
        ld_x.append(r.dma(("sync", "act")[h2], lambda e, h2_=h2: e.dma_start(
            out=X.rearrange("p (c t) -> p c t", t=T)[:, h2_ * 4:(h2_ + 1) * 4, :],
            in_=x_d.rearrange("(c p) t -> p c t", p=128)[:, h2_ * 4:(h2_ + 1) * 4, :]),
            chan="xio%d" % h2))

    small = []

    def gload(dst, src):
        o = r.dma("pool", lambda e, d=dst, s=src: e.dma_start(out=d, in_=s), chan="const")
        small.append(o)
        return o

    for dst, src in [(sb1c[:], sb1c_d[:]), (sb3c[:], sb3c_d[:]), (b1c[:], b1c_d[:]),
                     (sb2r[:], sb2r_d[:]),
                     (identb[:], identb_d[:]), (identf[:], identf_d[:]),
                     (b2ones[:], b2ones_d[:]), (onesr[:], onesr_d[:]), (onesc[:], onesc_d[:]),
                     (epsln[:], epsln_d[:]), (epsr2[:], epsr2_d[:]),
                     (wqrt[:], wqrt_d[:]), (wkrt[:], wkrt_d[:]), (wot[:], wot_d[:])]:
        gload(dst, src)
    ld_knw2 = gload(knw2[:], knw2_d[:])
    ld_qnw2 = gload(qnw2[:], qnw2_d[:])
    ld_wdq = gload(wdq.rearrange("p (kc d) -> p kc d", d=64),
                   wdq_d.rearrange("(kc p) d -> p kc d", p=128))
    ld_wuqt = gload(wuqt.rearrange("p (kc d) -> p kc d", d=64),
                    wuqt_d.rearrange("(kc p) d -> p kc d", p=128))
    ld_wdkv = gload(wdkv.rearrange("p (kc d) -> p kc d", d=64),
                    wdkv_d.rearrange("(kc p) d -> p kc d", p=128))
    ld_wukvt = gload(wukvt.rearrange("p (kc d) -> p kc d", d=128),
                     wukvt_d.rearrange("(kc p) d -> p kc d", p=128))
    consts = [small[-1]]

    def fold_q(e):
        for kc in range(4):
            i = e.matmul(PS_D[0:64, 0:64], wdq[:, kc * 64:(kc + 1) * 64],
                         wuqt[:, kc * 64:(kc + 1) * 64], start=(kc == 0), stop=(kc == 3))
        return i
    o_foldq = r.pe(fold_q, deps=consts)

    def fold_kv(e):
        for kc in range(8):
            i = e.matmul(PS_D[0:64, 512:640], wdkv[:, kc * 64:(kc + 1) * 64],
                         wukvt[:, kc * 128:(kc + 1) * 128], start=(kc == 0), stop=(kc == 7))
        return i
    o_foldkv = r.pe(fold_kv, deps=consts)

    def wqt_ev(e):
        e.copy(wqt[0:64, :], PS_D[0:64, 0:64])
        return e.copy(wqt[64:128, :], PS_D[0:64, 0:64])
    o_wqt = r.act(wqt_ev, deps=[o_foldq])

    def wkvt_ev(e):
        e.copy(wkvt[0:64, :], PS_D[0:64, 512:640])
        return e.copy(wkvt[64:128, :], PS_D[0:64, 512:640])
    o_wkvt = r.act(wkvt_ev, deps=[o_foldkv])
    o_qnw = r.dve(lambda e: e.tensor_scalar_mul(qnw2b[:], qnw2[:], 0.125), deps=consts)
    o_knw = r.dve(lambda e: e.tensor_copy(knw2b[:], knw2[:]), deps=consts)
    o_negon = r.pool(lambda e: e.memset(negon[:], -1.0))
    attn_wready = [o_wqt, o_wkvt, o_qnw, o_knw, o_negon]

    LN_SUM = PS_C[0:1, 0:512]
    LN_SUMSQ = PS_C[0:1, 512:1024]
    LN_A = PS_A[:, 0:512]
    LN_B = PS_A[:, 512:1024]

    def emit_ln(start_deps):
        group_done = []
        prev_stats = {0: None, 1: None}
        prev_apply = {0: [], 1: []}
        prev_rowrd = {0: None, 1: None}
        prev_evac = {0: None, 1: None}
        stage2 = {}
        for g in range(5):
          if g < 4:
            gofs = g * 512
            gp = g % 2
            ro = gp * 512
            XBF = XBF2[gp]
            XSQ = XSQ2[gp]
            Asb = Asb2[gp]
            Bsb = Bsb2[gp]
            SUM = PS_C[0:1, 0:512] if gp == 0 else PS_D[0:1, 0:512]
            SUMSQ = PS_C[0:1, 512:1024] if gp == 0 else PS_D[0:1, 512:1024]
            BA = PS_A[:, 0:512] if gp == 0 else PS_B[:, 0:512]
            BB = PS_A[:, 512:1024] if gp == 0 else PS_B[:, 512:1024]
            casts = []
            sqs = []
            for c in range(NCH):
                xin = X[:, c * T + gofs: c * T + gofs + 512]
                xb = XBF[:, c * 512:(c + 1) * 512]
                xs = XSQ[:, c * 512:(c + 1) * 512]
                war = [prev_stats[gp]] if prev_stats[gp] else []
                eng = ("dve", "act")[c % 2]
                o_c = r.op(eng, lambda e, a_=xb, b_=xin: _copy(e, a_, b_),
                           deps=list(start_deps) + war)
                o_s = r.dve(lambda e, a_=xs, b_=xb: e.tensor_mul(a_, b_, b_),
                            deps=[o_c] + war)
                casts.append(o_c)
                sqs.append(o_s)

            def stats(e, XBF_=XBF, XSQ_=XSQ, SUM_=SUM, SUMSQ_=SUMSQ):
                for c in range(NCH):
                    e.matmul(SUM_, onesc[:], XBF_[:, c * 512:(c + 1) * 512],
                             start=(c == 0), stop=(c == NCH - 1))
                for c in range(NCH):
                    i = e.matmul(SUMSQ_, onesc[:], XSQ_[:, c * 512:(c + 1) * 512],
                                 start=(c == 0), stop=(c == NCH - 1))
                return i
            o_stats = r.pe(stats, deps=casts + sqs +
                           ([prev_rowrd[gp]] if prev_rowrd[gp] else []) + list(start_deps))
            prev_stats[gp] = o_stats

            stage2[g] = (gp, ro, XBF, Asb, Bsb, BA, BB, SUM, SUMSQ, o_stats, casts, gofs)
          if g - 1 >= 0:
            (gp, ro, XBF, Asb, Bsb, BA, BB, SUM, SUMSQ, o_stats, casts, gofs) = stage2[g - 1]
            o_m = r.act(lambda e, S_=SUM, ro_=ro: e.activation(
                m_row[0:1, ro_:ro_ + 512], S_, AF.Copy, scale=1.0 / D), deps=[o_stats])
            o_m2 = r.act(lambda e, S_=SUMSQ, ro_=ro: e.activation(
                ms2_row[0:1, ro_:ro_ + 512], S_, AF.Copy, scale=1.0 / D), deps=[o_stats])
            prev_rowrd[gp] = o_m2
            o_v1 = r.dve(lambda e, ro_=ro: e.tensor_mul(
                var_row[0:1, ro_:ro_ + 512], m_row[0:1, ro_:ro_ + 512],
                m_row[0:1, ro_:ro_ + 512]), deps=[o_m])
            o_v2 = r.dve(lambda e, ro_=ro: e.tensor_sub(
                var_row[0:1, ro_:ro_ + 512], ms2_row[0:1, ro_:ro_ + 512],
                var_row[0:1, ro_:ro_ + 512]), deps=[o_v1, o_m2], fence=True)
            o_ln = r.act(lambda e, ro_=ro: e.activation(
                a_row[0:1, ro_:ro_ + 512], var_row[0:1, ro_:ro_ + 512],
                AF.Ln, bias=epsln[:]), deps=[o_v2])
            o_ex = r.act(lambda e, ro_=ro: e.activation(
                a_row[0:1, ro_:ro_ + 512], a_row[0:1, ro_:ro_ + 512],
                AF.Exp, scale=-0.5), deps=[o_ln])
            o_b = r.dve(lambda e, ro_=ro: e.scalar_tensor_tensor(
                b_row[0:1, ro_:ro_ + 512], m_row[0:1, ro_:ro_ + 512], -1.0,
                a_row[0:1, ro_:ro_ + 512], ALU.mult, ALU.mult), deps=[o_ex])
            o_ab = r.dve(lambda e, ro_=ro: e.tensor_copy(
                abf_row[0:1, ro_:ro_ + 512], a_row[0:1, ro_:ro_ + 512]), deps=[o_ex])
            o_bb = r.dve(lambda e, ro_=ro: e.tensor_copy(
                bbf_row[0:1, ro_:ro_ + 512], b_row[0:1, ro_:ro_ + 512]),
                deps=[o_b], fence=True)

            def bcast(e, BA_=BA, BB_=BB, ro_=ro):
                e.matmul(BA_, onesr[0:1, 0:128], abf_row[0:1, ro_:ro_ + 512],
                         start=True, stop=True)
                return e.matmul(BB_, onesr[32:33, 0:128], bbf_row[0:1, ro_:ro_ + 512],
                                start=True, stop=True)
            o_bc = r.pe(bcast, deps=[o_ab, o_bb] +
                        ([prev_evac[gp]] if prev_evac[gp] else []))
            o_Ae = r.act(lambda e, A_=Asb, BA_=BA: e.copy(A_[:], BA_),
                         deps=[o_bc] + prev_apply[gp][-1:])
            o_Be = r.act(lambda e, B_=Bsb, BB_=BB: e.copy(B_[:], BB_),
                         deps=[o_bc] + prev_apply[gp][-1:])
            prev_evac[gp] = o_Be

            applies = []
            for c in range(NCH):
                xin = X[:, c * T + gofs: c * T + gofs + 512]
                xh = XH[:, c * T + gofs: c * T + gofs + 512]
                o_t = r.dve(lambda e, a_=xin, A_=Asb: e.tensor_mul(TMUL[:], a_, A_[:]),
                            deps=[o_Ae])
                o_a = r.dve(lambda e, dd=xh, B_=Bsb: e.tensor_add(dd, TMUL[:], B_[:]),
                            deps=[o_t, o_Be])
                applies.append(o_a)
            prev_apply[gp] = applies
            group_done.append(applies)
        tails = [prev_stats[0], prev_stats[1], prev_evac[0], prev_evac[1]] + \
            group_done[-1][-2:] + group_done[-2][-2:]
        return group_done, tails

    RES = []
    for e_ in range(E):
        r0, r1 = (3 * e_) % 8, (3 * e_ + 3) % 8
        RES.append((min(r0, r1), abs(r1 - r0)))

    UB = [[PS_A[:, 0:512], PS_A[:, 512:1024], PS_B[:, 0:512], PS_B[:, 512:1024]],
          [PS_C[:, 0:512], PS_C[:, 512:1024], PS_D[:, 0:512], PS_D[:, 512:1024]]]
    HB4 = [PS_A[:, 0:512], PS_A[:, 512:1024], PS_B[:, 0:512], PS_B[:, 512:1024]]
    YB2 = [PS_C[:, 0:512], PS_C[:, 512:1024]]

    def xp_rhs(c, a_, d_):
        return XP[:, c * T:(c + 1) * T].rearrange(
            "p (gh r o) -> p gh r o", gh=2, r=8)[:, :, a_:a_ + d_ + 1:d_, :]

    def x_scat(c, gh, a_, d_):
        base = c * T + gh * 1024
        return X[:, base: base + 1024].rearrange(
            "p (o i) -> p o i", i=8)[:, :, a_:a_ + d_ + 1:d_].transpose([0, 2, 1])

    scr_sts = {}

    def emit_ffn(xh_groups, start_deps, dma_deps, sw2_deps=None, scratch=None):
        ld_sw2_holder = []

        def rec_sw2():
            ld_sw2_holder.append(r.dma("pool", lambda e: e.dma_start(
                out=SW2.rearrange("p (hc c) -> p hc c", c=1024),
                in_=sw2_d.rearrange("(hc p) c -> p hc c", p=128)),
                deps=(sw2_deps if sw2_deps is not None else dma_deps), chan="sw2"))

        up_pe = {}
        glue = {}
        g_last = {}
        down_war = []
        down_done = []
        all_gel = []

        for half in range(2):
            hofs = half * 1024
            xh_dep = xh_groups[half * 2] + xh_groups[half * 2 + 1]
            g_ready = {0: [], 1: []}
            for h in range(16):
                ih = half * 16 + h
                par = ih % 2
                war_up = list(glue.get(ih - 2, []))
                if h < 2:
                    war_up = war_up + down_war + list(start_deps)
                ld1 = r.dma("pool", lambda e, p_=par, h_=h: e.dma_start(
                    out=SW1s[p_].rearrange("p (kc n) -> p kc n", n=128),
                    in_=sw1_d.rearrange("(kc p) n -> p kc n", p=128)[:, :, h_ * 128:(h_ + 1) * 128]),
                    deps=[up_pe.get(ih - 2)] + list(dma_deps), chan="sw1p%d" % par)
                ld3 = r.dma("pool", lambda e, p_=par, h_=h: e.dma_start(
                    out=SW3s[p_].rearrange("p (kc n) -> p kc n", n=128),
                    in_=sw3_d.rearrange("(kc p) n -> p kc n", p=128)[:, :, h_ * 128:(h_ + 1) * 128]),
                    deps=[up_pe.get(ih - 2)] + list(dma_deps), chan="sw3p%d" % par)

                def up(e, p_=par, hofs_=hofs):
                    for g2 in range(2):
                        for kc in range(NCH):
                            e.matmul(UB[p_][2 * g2], SW1s[p_][:, kc * 128:(kc + 1) * 128],
                                     XH[:, kc * T + hofs_ + g2 * 512: kc * T + hofs_ + g2 * 512 + 512],
                                     start=(kc == 0), stop=(kc == NCH - 1))
                        for kc in range(NCH):
                            i = e.matmul(UB[p_][2 * g2 + 1], SW3s[p_][:, kc * 128:(kc + 1) * 128],
                                         XH[:, kc * T + hofs_ + g2 * 512: kc * T + hofs_ + g2 * 512 + 512],
                                         start=(kc == 0), stop=(kc == NCH - 1))
                    return i
                if ih == 1:
                    rec_sw2()
                o_up = r.pe(up, deps=[ld1, ld3] + xh_dep + war_up)
                up_pe[ih] = o_up

                gl = []
                for g2 in range(2):
                    t1war = [g_last.get((par, 1 - g2))] if g2 == 0 else [gl[-1]]
                    o_t1 = r.act(lambda e, p_=par, g2_=g2, h_=h: e.activation(
                        T1b[p_][:], UB[p_][2 * g2_], AF.Silu, bias=sb1c[:, h_:h_ + 1]),
                        deps=[o_up] + t1war)
                    o_t3 = r.dve(lambda e, p_=par, g2_=g2, h_=h: e.tensor_scalar_add(
                        T3b[p_][:], UB[p_][2 * g2_ + 1], sb3c[:, h_:h_ + 1]),
                        deps=[o_up] + t1war)
                    o_g = r.dve(lambda e, p_=par, g2_=g2, h_=h: e.tensor_mul(
                        Gh[:, h_ * 1024 + g2_ * 512: h_ * 1024 + g2_ * 512 + 512],
                        T1b[p_][:], T3b[p_][:]), deps=[o_t1, o_t3])
                    gl += [o_t1, o_t3, o_g]
                    g_last[(par, g2)] = o_g
                    g_ready[g2].append(o_g)
                glue[ih] = gl

            prev_comb = {}
            last_upg = [up_pe[half * 16 + 15]] + glue[half * 16 + 15] + glue[half * 16 + 14]
            for g2 in range(2):
                for c in range(NCH):
                    yb = [PS_A[:, 0:512], PS_B[:, 0:512]][c % 2]

                    def down(e, g2_=g2, c_=c, yb_=yb):
                        for hc in range(16):
                            e.matmul(yb_, SW2[:, hc * 1024 + c_ * 128: hc * 1024 + c_ * 128 + 128],
                                     Gh[:, hc * 1024 + g2_ * 512: hc * 1024 + g2_ * 512 + 512],
                                     start=(hc == 0), stop=False)
                        return e.matmul(yb_, sb2r[0:1, c_ * 128:(c_ + 1) * 128],
                                        onesr[0:1, 0:512], start=False, stop=True)
                    o_y = r.pe(down, deps=ld_sw2_holder + g_ready[g2] + last_upg +
                               ([prev_comb[c % 2]] if c % 2 in prev_comb else []))
                    o_cb = r.dve(lambda e, g2_=g2, c_=c, yb_=yb, hofs_=hofs: e.scalar_tensor_tensor(
                        X[:, c_ * T + hofs_ + g2_ * 512: c_ * T + hofs_ + g2_ * 512 + 512],
                        yb_, 0.25,
                        X[:, c_ * T + hofs_ + g2_ * 512: c_ * T + hofs_ + g2_ * 512 + 512],
                        ALU.mult, ALU.add), deps=[o_y] + list(start_deps))
                    prev_comb[c % 2] = o_cb
                    down_done.append(o_cb)
            down_war = [prev_comb.get(0), prev_comb.get(1)]

        last_down = [o_y]
        pack_ops = []
        for c in range(NCH):
            for gh in range(2):
                base = c * T + gh * 1024

                def packf(e, base_=base):
                    return _copy(e,
                        XP[:, base_: base_ + 1024].rearrange("p (r o) -> p r o", r=8),
                        XH[:, base_: base_ + 1024].rearrange("p (o i) -> p o i", i=8).transpose([0, 2, 1]))
                pack_ops.append(r.op("dve" if (c + gh) % 2 == 0 else "act", packf,
                                     deps=last_down +
                                     xh_groups[gh * 2] + xh_groups[gh * 2 + 1]))

        w1q_rd = {0: list(down_war), 1: list(down_war)}
        hb_rd = {j: list(down_war) for j in range(4)}
        yb_rd = {0: down_done[-1:], 1: down_done[-1:]}
        ldq = {}

        w1q_st = {0: None, 1: None}
        w2_st = {0: None, 1: None}

        def load_w1q(qg):
            e_, q = qg // 4, qg % 4
            deps = list(w1q_rd[qg % 2]) if qg >= 2 else list(dma_deps)
            if scratch == "save" and w1q_st[qg % 2] is not None:
                deps = deps + [w1q_st[qg % 2]]
            parts = []
            if scratch == "load":
                for h2 in range(2):
                    parts.append(r.dma(("sync", "act")[h2], lambda e, eq=qg, b_=qg % 2, h2_=h2: e.dma_start(
                        out=W1q[b_][:, h2_ * 1024:(h2_ + 1) * 1024],
                        in_=sw1s_d[eq, :, h2_ * 1024:(h2_ + 1) * 1024]),
                        deps=deps, chan="w1qL%d%s" % (qg % 2, "ab"[h2])))
            else:
                for h2 in range(2):
                    parts.append(r.dma("pool", lambda e, e2=e_, q_=q, b_=qg % 2, h2_=h2: e.dma_start(
                        out=W1q[b_].rearrange("p (kc n) -> p kc n", n=256)[:, h2_ * 4:(h2_ + 1) * 4, :],
                        in_=w1_d[e2].rearrange("(kc p) n -> p kc n", p=128)[:, h2_ * 4:(h2_ + 1) * 4, q_ * 256:(q_ + 1) * 256]),
                        deps=deps, chan="w1q%d%s" % (qg % 2, "ab"[h2])))
            ldq[qg] = parts
            if scratch == "save":
                w1q_st[qg % 2] = r.dma("sync", lambda e, eq=qg, b_=qg % 2: e.dma_start(
                    out=sw1s_d[eq], in_=W1q[b_][:, :]), deps=parts, chan="stw1")
                scr_sts["w1"] = w1q_st[qg % 2]

        load_w1q(0)
        load_w1q(1)
        ld_w2 = {}
        ld_b2 = {}
        last_md = {}

        def load_w2(e_):
            dep = [last_md.get(e_ - 2)] if e_ >= 2 else list(down_war) + list(ldq[0])
            if scratch == "save" and w2_st[e_ % 2] is not None:
                dep = dep + [w2_st[e_ % 2]]
            ld_b2[e_] = r.dma("pool", lambda e, e2=e_, b_=e_ % 2: e.dma_start(
                out=b2s[b_], in_=b2r_d[0:1, e2 * D:(e2 + 1) * D]), deps=dep + down_done[-1:],
                chan="b2_%d" % (e_ % 2))
            parts = []
            if scratch == "load":
                for h2 in range(2):
                    parts.append(r.dma(("pool", "sync")[h2],
                        lambda e, e2=e_, b_=e_ % 2, h2_=h2: e.dma_start(
                        out=W2s[b_][:, h2_ * 4096:(h2_ + 1) * 4096],
                        in_=sw2s_d[e2, :, h2_ * 4096:(h2_ + 1) * 4096]),
                        deps=dep, chan="w2L%d%s" % (e_ % 2, "ps"[h2])))
            else:
                for h2 in range(4):
                    parts.append(r.dma("pool",
                        lambda e, e2=e_, b_=e_ % 2, h2_=h2: e.dma_start(
                        out=W2s[b_].rearrange("p (kc n) -> p kc n", n=D)[:, h2_ * 2:(h2_ + 1) * 2, :],
                        in_=w2_d[e2].rearrange("(kc p) n -> p kc n", p=128)[:, h2_ * 2:(h2_ + 1) * 2, :]),
                        deps=dep, chan="w2_%d%s" % (e_ % 2, "ps"[h2 % 2])))
            ld_w2[e_] = parts
            if scratch == "save":
                w2_st[e_ % 2] = r.dma("sync", lambda e, e2=e_, b_=e_ % 2: e.dma_start(
                    out=sw2s_d[e2], in_=W2s[b_][:, :]), deps=parts, chan="stw2")
                scr_sts["w2"] = w2_st[e_ % 2]

        load_w2(0)
        load_w2(1)

        scat_all = []
        for e_ in range(E):
            par = e_ % 2
            a_, d_ = RES[e_]
            gels = []
            o_mu = None
            for hr in range(8):
                qg = e_ * 4 + hr // 2
                slot = hr % 4

                def moe_up_f(e, b_=qg % 2, hr_=hr, a2=a_, d2=d_, slot_=slot):
                    hb = HB4[slot_]
                    for kc in range(NCH):
                        i = e.matmul(hb, W1q[b_][:, kc * 256 + (hr_ % 2) * 128: kc * 256 + (hr_ % 2) * 128 + 128],
                                     xp_rhs(kc, a2, d2),
                                     start=(kc == 0), stop=(kc == NCH - 1))
                    return i
                o_mu = r.pe(moe_up_f, deps=list(ldq[qg]) + pack_ops + hb_rd[slot] + down_done[-1:])
                o_gl = r.act(lambda e, hr_=hr, e2=e_, slot_=slot: e.activation(
                    Hs2[:, hr_ * 512:(hr_ + 1) * 512], HB4[slot_], AF.Gelu,
                    bias=b1c[:, e2 * 8 + hr_: e2 * 8 + hr_ + 1]), deps=[o_mu])
                hb_rd[slot] = [o_gl]
                gels.append(o_gl)
                all_gel.append(o_gl)
                if hr % 2 == 1:
                    w1q_rd[qg % 2] = [o_mu]
                    if qg + 2 < E * 4:
                        load_w1q(qg + 2)

            o_md = None
            for c in range(NCH):
                yb = YB2[c % 2]

                def moe_dn(e, e2=e_, c_=c, b_=par, yb_=yb):
                    for hc in range(8):
                        e.matmul(yb_, W2s[b_][:, hc * D + c_ * 128: hc * D + c_ * 128 + 128],
                                 Hs2[:, hc * 512:(hc + 1) * 512],
                                 start=(hc == 0), stop=False)
                    return e.matmul(yb_, b2s[b_][0:1, c_ * 128:(c_ + 1) * 128],
                                    onesr[0:1, 0:512], start=False, stop=True)
                o_md = r.pe(moe_dn, deps=list(ld_w2[e_]) + [ld_b2[e_]] + gels + yb_rd[c % 2])

                def scat(e, c_=c, yb_=yb, a2=a_, d2=d_):
                    for gh in range(2):
                        xv = x_scat(c_, gh, a2, d2)
                        yv = yb_[:, gh * 256:(gh + 1) * 256].rearrange(
                            "p (r o) -> p r o", r=2)
                        i = e.scalar_tensor_tensor(xv, yv, 0.125, xv, ALU.mult, ALU.add)
                    return i
                o_sc = r.dve(scat, deps=[o_md] + down_done[-1:])
                yb_rd[c % 2] = [o_sc]
                scat_all.append(o_sc)
            last_md[e_] = o_md
            if e_ + 2 < E:
                load_w2(e_ + 2)

        tails = [last_md[E - 1], scat_all[-1], scat_all[-2], all_gel[-1],
                 down_done[-1], up_pe[31]]
        if scratch == "save":
            tails = tails + [w1q_st[0], w1q_st[1], w2_st[0], w2_st[1]]
        return tails

    def emit_attn(xh_groups, start_deps):
        xh_all = [op for g_ in xh_groups for op in g_]
        inner_tails = {0: [], 1: []}
        innerA_rd = []
        innerC_rd = []
        innerD_rd = []
        last_xadd = []
        final_exps = []
        final_misc = []

        p_war = {0: [], 1: [], 2: []}
        ptsb_war = {0: [], 1: []}
        ctx_war = {0: [], 1: []}
        rs_war = {0: [], 1: []}
        ptev = {0: None, 1: None}
        sq_warA = []
        sq_warB = []
        trslot_war = []

        for p in range(8):
            par = p % 2
            K2v, Q2v, V2v = K2[par], Q2[par], V2[par]
            AOv = AO2[par]
            pair_dep = list(start_deps) + attn_wready + xh_all + inner_tails[par]

            proj_done = []
            v_done = []
            prev_ln = None
            prev_pe3 = None
            prev_kv_rd = []
            prev_qc_rd = []
            prev_scrd = []
            prev_qr = None
            prev_vev = None
            prev_kr = None
            for g in range(4):
                go = g * 512

                def pe1(e, p_=p, go_=go):
                    xh0 = XH[0:64, p_ * T + go_: p_ * T + go_ + 512]
                    xh1 = XH[64:128, p_ * T + go_: p_ * T + go_ + 512]
                    e.matmul(PS_A[:, 0:512], wkvt[0:64, :], xh0, start=True, stop=True)
                    e.matmul(PS_A[:, 512:1024], wkvt[64:128, :], xh1, start=True, stop=True)
                    e.matmul(PS_B[0:64, 0:512], wqt[0:64, :], xh0, start=True, stop=True)
                    e.matmul(PS_B[0:64, 512:1024], wqt[64:128, :], xh1, start=True, stop=True)
                    e.matmul(PS_C[0:64, 0:512], wkrt[0:64, :], xh0, start=True, stop=True)
                    return e.matmul(PS_C[0:64, 512:1024], wkrt[64:128, :], xh1, start=True, stop=True)
                war_abc = prev_scrd + ([prev_qr, prev_vev, prev_kr] if prev_qr else [])
                o_pe1 = r.pe(pe1, deps=pair_dep + war_abc + innerA_rd + innerC_rd)

                o_kv = r.dve(lambda e: e.tensor_copy(kvsb[:], PS_A[:, 0:1024]),
                             deps=[o_pe1] + prev_kv_rd)
                o_qc = r.dve(lambda e: e.tensor_copy(qcsb[:], PS_B[0:64, 0:1024]),
                             deps=[o_pe1] + prev_qc_rd)
                o_kr = r.act(lambda e, K2v_=K2v, go_=go: e.copy(
                    K2v_[64:128, :].rearrange("p (hh t) -> p hh t", hh=2)[:, :, go_:go_ + 512],
                    PS_C[0:64, 0:1024]), deps=[o_pe1])
                prev_kr = o_kr

                o_s1 = r.act(lambda e: e.activation(sqk[0:64, :], PS_A[0:64, 0:512],
                                                    AF.Square), deps=[o_pe1])
                o_s2 = r.act(lambda e: e.activation(sqk[64:128, :], PS_A[0:64, 512:1024],
                                                    AF.Square), deps=[o_pe1])
                o_s3 = r.act(lambda e: e.activation(sqq[0:64, :], PS_B[0:64, 0:512],
                                                    AF.Square), deps=[o_pe1])
                o_s4 = r.act(lambda e: e.activation(sqq[64:128, :], PS_B[0:64, 512:1024],
                                                    AF.Square), deps=[o_pe1])

                def pe2(e):
                    e.matmul(PS_D[0:2, 0:512], b2ones[:], sqk[:], start=True, stop=True)
                    return e.matmul(PS_D[0:2, 512:1024], b2ones[:], sqq[:], start=True, stop=True)
                o_pe2 = r.pe(pe2, deps=[o_s1, o_s2, o_s3, o_s4, prev_ln] + innerD_rd)

                o_ln2 = r.act(lambda e: e.activation(row2[:], PS_D[0:2, 0:1024], AF.Ln,
                                                     scale=1.0 / 64, bias=epsr2[:]), deps=[o_pe2])
                prev_ln = o_ln2
                o_ex2 = r.act(lambda e: e.activation(rstd2[:], row2[:], AF.Exp, scale=-0.5),
                              deps=[o_ln2, prev_pe3])

                def pe3(e):
                    e.matmul(PS_A[:, 0:512], knw2b[:], rstd2[:, 0:512], start=True, stop=True)
                    return e.matmul(PS_A[:, 512:1024], qnw2b[:], rstd2[:, 512:1024],
                                    start=True, stop=True)
                o_pe3 = r.pe(pe3, deps=[o_ex2, o_kv, o_s1, o_s2])
                prev_pe3 = o_pe3

                o_k0 = r.dve(lambda e, K2v_=K2v, go_=go: e.tensor_mul(
                    K2v_[0:64, go_:go_ + 512], kvsb[0:64, 0:512], PS_A[0:64, 0:512]),
                    deps=[o_pe3])
                o_k1 = r.dve(lambda e, K2v_=K2v, go_=go: e.tensor_mul(
                    K2v_[0:64, T + go_: T + go_ + 512], kvsb[0:64, 512:1024], PS_A[64:128, 0:512]),
                    deps=[o_pe3])
                o_q0 = r.dve(lambda e, Q2v_=Q2v, go_=go: e.tensor_mul(
                    Q2v_[0:64, go_:go_ + 512], qcsb[0:64, 0:512], PS_A[0:64, 512:1024]),
                    deps=[o_pe3])
                o_q1 = r.dve(lambda e, Q2v_=Q2v, go_=go: e.tensor_mul(
                    Q2v_[0:64, T + go_: T + go_ + 512], qcsb[0:64, 512:1024], PS_A[64:128, 512:1024]),
                    deps=[o_pe3])
                prev_scrd = [o_k0, o_k1, o_q0, o_q1]

                def pe4(e, Q2v_=Q2v, go_=go):
                    e.matmul(PS_B[0:64, 0:512], wqrt[:], Q2v_[0:64, go_:go_ + 512],
                             start=True, stop=True)
                    e.matmul(PS_B[0:64, 512:1024], wqrt[:], Q2v_[0:64, T + go_: T + go_ + 512],
                             start=True, stop=True)
                    for hh in range(2):
                        for tl in range(4):
                            i = e.matmul(PS_C[:, (hh * 4 + tl) * 64:(hh * 4 + tl) * 64 + 64],
                                         kvsb[64:128, hh * 512 + tl * 128: hh * 512 + tl * 128 + 128],
                                         wot[64:128, :], start=True, stop=True)
                    return i
                o_pe4 = r.pe(pe4, deps=[o_q0, o_q1, o_qc, o_kr, o_kv, o_s3, o_s4])

                o_qr = r.act(lambda e, Q2v_=Q2v, go_=go: e.copy(
                    Q2v_[64:128, :].rearrange("p (hh t) -> p hh t", hh=2)[:, :, go_:go_ + 512],
                    PS_B[0:64, 0:1024]), deps=[o_pe4])
                prev_qr = o_qr
                o_vev = r.dve(lambda e, V2v_=V2v, g_=g: e.tensor_copy(
                    V2v_.rearrange("p (hh tt o) -> p hh tt o", hh=2, tt=16)[:, :, g_ * 4:(g_ + 1) * 4, 0:64],
                    PS_C[:, 0:512].rearrange("p (hh tl o) -> p hh tl o", hh=2, tl=4)),
                    deps=[o_pe4])
                prev_vev = o_vev
                prev_kv_rd = [o_pe4, o_k0, o_k1, o_s1, o_s2]
                prev_qc_rd = [o_q0, o_q1, o_s3, o_s4]
                proj_done += [o_k0, o_k1, o_q0, o_q1, o_qr, o_kr]
                v_done.append(o_vev)

            projC_rd = prev_vev
            projD_rd = prev_ln

            bias_ops = {}
            for hh in range(2):
                nb = 584 + hh * 32
                o_sq = r.dve(lambda e, hh_=hh, Q_=Q2v: e.tensor_mul(
                    SQA, Q_[:, hh_ * T:(hh_ + 1) * T], Q_[:, hh_ * T:(hh_ + 1) * T]),
                    deps=proj_done + sq_warA)
                o_sk = r.dve(lambda e, hh_=hh, K_=K2v: e.tensor_mul(
                    SQB, K_[:, hh_ * T:(hh_ + 1) * T], K_[:, hh_ * T:(hh_ + 1) * T]),
                    deps=proj_done + sq_warB)

                def normq(e, nb_=nb):
                    for t in range(16):
                        i = e.matmul(PS_D[:, nb_ + t: nb_ + t + 1],
                                     SQA[:, t * 128:(t + 1) * 128], onesc[:],
                                     start=True, stop=True)
                    return i
                o_nq = r.pe(normq, deps=[o_sq, prev_ln] + ctx_war[1])

                def normk(e, nb_=nb):
                    for t in range(16):
                        i = e.matmul(PS_D[:, nb_ + 16 + t: nb_ + 17 + t],
                                     SQB[:, t * 128:(t + 1) * 128], onesc[:],
                                     start=True, stop=True)
                    return i
                o_nk = r.pe(normk, deps=[o_sk, prev_ln] + ctx_war[1])
                sq_warA = [o_nq]
                sq_warB = [o_nk]
                o_rq = r.dve(lambda e, hh_=hh, nb_=nb: e.reduce_max(
                    bqk[:, hh_ * 2: hh_ * 2 + 1], PS_D[:, nb_: nb_ + 16], axis=AX.X),
                    deps=[o_nq])
                o_rk = r.dve(lambda e, hh_=hh, nb_=nb: e.reduce_max(
                    bqk[:, hh_ * 2 + 1: hh_ * 2 + 2], PS_D[:, nb_ + 16: nb_ + 32], axis=AX.X),
                    deps=[o_nk])
                o_t = r.pe(lambda e, hh_=hh: e.transpose(
                    PS_D[0:2, 840:968], bqk[:, hh_ * 2: hh_ * 2 + 2], identf[:]),
                    deps=[o_rq, o_rk] + trslot_war)
                o_r2 = r.dve(lambda e: e.reduce_max(
                    bqk[0:2, 4:5], PS_D[0:2, 840:968], axis=AX.X), deps=[o_t])
                o_ln = r.act(lambda e: e.activation(
                    lnc[:], bqk[0:2, 4:5], AF.Ln), deps=[o_r2])
                o_ps = r.pe(lambda e, hh_=hh: e.matmul(
                    PS_D[0:1, 968 + hh_: 969 + hh_], onesc[0:2, :], lnc[:],
                    start=True, stop=True), deps=[o_ln])
                o_ex = r.act(lambda e, hh_=hh: e.activation(
                    mbf[0:1, hh_:hh_ + 1], PS_D[0:1, 968 + hh_: 969 + hh_],
                    AF.Exp, scale=0.5), deps=[o_ps])
                trslot_war = [o_r2, o_ps]
                o_bb = r.pe(lambda e, hh_=hh: e.matmul(
                    PS_D[:, 972 + hh_: 973 + hh_], negon[:], mbf[0:1, hh_:hh_ + 1],
                    start=True, stop=True), deps=[o_ex])
                o_bf = r.act(lambda e, hh_=hh, bc_=par * 2 + hh: e.copy(
                    mxc[:, bc_:bc_ + 1], PS_D[:, 972 + hh_: 973 + hh_]), deps=[o_bb])
                bias_ops[hh] = o_bf

            def vones(e, V2v_=V2v):
                e.memset(V2v_.rearrange("p (tt o) -> p tt o", o=68)[:, :, 64:65], 1.0)
                return e.memset(V2v_.rearrange("p (tt o) -> p tt o", o=68)[:, :, 65:68], 0.0)
            o_ones = r.pool(vones, deps=pair_dep)

            RPAR = [
                dict(S=PS_A, PTps=PS_C[:, 0:512].bitcast(BF16), PTsb=PTs[:, 0:1024]),
                dict(S=PS_B, PTps=PS_D[:, 0:512].bitcast(BF16), PTsb=PTs[:, 1024:2048]),
            ]
            PSLOT = [Psb[:, 0:1024], Psb[:, 1024:2048], P3]
            CTXs = [PS_C[:, 512:580], PS_D[:, 512:580]]
            s_war = {0: list(proj_done), 1: list(proj_done)}
            resc_last = [None]
            pv_ref = {}
            soft_ref = {}
            evs_ref = {}

            tasks0 = []
            for qi in range(16):
                kl = 128 * (qi + 1)
                nch = (kl + 1023) // 1024
                for j in range(nch):
                    tasks0.append((qi, j, min(1024, kl - j * 1024), nch))

            ao_done = []
            for hh in range(2):
                ntk = len(tasks0)
                for m in range(ntk + 4):
                    if 0 <= m - 3 < ntk:
                        mm2 = m - 3
                        qi2, j2, w2, nch2 = tasks0[mm2]
                        par2 = mm2 % 2
                        R2 = RPAR[par2]
                        Pv2 = PSLOT[mm2 % 3]
                        nblkj = w2 // 128
                        exps2 = soft_ref[mm2]

                        def trs(e, R_=R2, P_=Pv2, nblkj_=nblkj):
                            for b in range(nblkj_):
                                i = e.transpose(R_["PTps"][:, b * 128:(b + 1) * 128],
                                                P_[:, b * 128:(b + 1) * 128], identb[:])
                            return i
                        o_tr = r.pe(trs, deps=exps2 +
                                    ([ptev[par2]] if ptev[par2] else []) +
                                    [projC_rd if par2 == 0 else projD_rd])
                        p_war[mm2 % 3] = [o_tr]
                        halves = [(0, min(4, nblkj))] + ([(4, nblkj)] if nblkj > 4 else [])
                        evs = []
                        for (b0, b1) in halves:
                            o_pt = r.op("dve",
                                        lambda e, R_=R2, b0_=b0, b1_=b1: _copy(
                                            e, R_["PTsb"][:, b0_ * 128:b1_ * 128],
                                            R_["PTps"][:, b0_ * 128:b1_ * 128]),
                                        deps=[o_tr] + ptsb_war[par2])
                            evs.append(o_pt)
                        ptev[par2] = evs[-1]
                        evs_ref[mm2] = (halves, evs)
                    if 0 <= m - 4 < ntk:
                        mm3 = m - 4
                        qi3, j3, w3, nch3 = tasks0[mm3]
                        par3 = mm3 % 2
                        R3 = RPAR[par3]
                        nblkj = w3 // 128
                        halves, evs = evs_ref[mm3]
                        cpar = qi3 % 2
                        ctx = CTXs[cpar]
                        pvs_ops = []
                        for hi, (b0, b1) in enumerate(halves):
                            def pvs(e, R_=R3, V2v_=V2v, hh_=hh, j_=j3, b0_=b0, b1_=b1,
                                    nch_=nch3, nblkj_=nblkj, ctx_=ctx):
                                for b in range(b0_, b1_):
                                    kb = j_ * 8 + b
                                    i = e.matmul(ctx_, R_["PTsb"][:, b * 128:(b + 1) * 128],
                                                 V2v_[:, hh_ * 1088 + kb * 68: hh_ * 1088 + kb * 68 + 68],
                                                 start=(j_ == 0 and b == 0),
                                                 stop=(j_ == nch_ - 1 and b == nblkj_ - 1))
                                return i
                            dep0 = ctx_war[cpar] if (j3 == 0 and hi == 0) else []
                            o_pv = r.pe(pvs, deps=[evs[hi], o_ones] + v_done + list(dep0))
                            pvs_ops.append(o_pv)
                        ptsb_war[par3] = pvs_ops
                        pv_ref[mm3] = o_pv
                        if j3 == nch3 - 1:
                            qb = qi3 % 2
                            ao = AOv[:, hh * 1024 + qi3 * 64: hh * 1024 + qi3 * 64 + 64]
                            o_rs = r.dve(lambda e, qb_=qb, ctx_=ctx: e.reciprocal(
                                rs[:, qb_:qb_ + 1], ctx_[:, 64:65]),
                                deps=[o_pv] + rs_war[qb])
                            o_resc = r.dve(lambda e, ao_=ao, qb_=qb, ctx_=ctx: e.tensor_scalar_mul(
                                ao_, ctx_[:, 0:64], rs[:, qb_:qb_ + 1]),
                                deps=[o_rs, o_pv], fence=True)
                            ctx_war[cpar] = [o_resc, o_rs]
                            rs_war[qb] = [o_resc]
                            resc_last[0] = o_resc
                    if m < ntk:
                        qi, j, w, nch = tasks0[m]
                        parm = m % 2
                        R = RPAR[parm]

                        def smm(e, R_=R, K2v_=K2v, Q2v_=Q2v, hh_=hh, qi_=qi, j_=j, w_=w):
                            for s in range(0, w_, 512):
                                n = min(512, w_ - s)
                                i = e.matmul(R_["S"][:, s:s + n],
                                             Q2v_[:, hh_ * T + qi_ * 128: hh_ * T + qi_ * 128 + 128],
                                             K2v_[:, hh_ * T + j_ * 1024 + s: hh_ * T + j_ * 1024 + s + n],
                                             start=True, stop=True)
                            return i
                        Pv = PSLOT[m % 3]
                        o_s = r.pe(smm, deps=s_war[parm])
                        o_e = r.op("act", lambda e, R_=R, P_=Pv, w_=w, bc_=par * 2 + hh: e.activation(
                            P_[:, 0:w_], R_["S"][:, 0:w_], AF.Exp,
                            bias=mxc[:, bc_:bc_ + 1]),
                            deps=[o_s, bias_ops[hh]] + p_war[m % 3], fence=True)
                        exp_ops = [o_e]
                        if j == nch - 1:
                            o_msk = r.pool(lambda e, P_=Pv, w_=w: e.affine_select(
                                P_[:, w_ - 128: w_], P_[:, w_ - 128: w_],
                                pattern=[[-1, 128]], compare_op=ALU.is_ge, fill=0.0,
                                base=0, channel_multiplier=1), deps=[o_e])
                            exp_ops.append(o_msk)
                        s_war[parm] = [o_e]
                        soft_ref[m] = exp_ops
                ao_done.append(resc_last[0])
            slot_rd = {0: [pv_ref[len(tasks0) - 1]], 1: [pv_ref[len(tasks0) - 1]]}
            slotT = [PS_C[0:64, 768:832].bitcast(BF16), PS_D[0:64, 768:832].bitcast(BF16)]
            xa_ops = []
            for k in range(32):
                hh_f, qi_f = k % 2, k // 2
                sl = k % 2
                ctxT = slotT[sl]

                def btr(e, hh_=hh_f, qi_=qi_f, ctxT_=ctxT, AOv_=AOv):
                    return e.transpose(ctxT_,
                                       AOv_[:, hh_ * 1024 + qi_ * 64: hh_ * 1024 + qi_ * 64 + 64],
                                       identb[:])
                o_btr = r.pe(btr, deps=ao_done + [x for x in slot_rd[sl] if x])
                o_xa = r.dve(lambda e, hh_=hh_f, qi_=qi_f, p_=p, ctxT_=ctxT: e.tensor_add(
                    X[64 * hh_: 64 * hh_ + 64, p_ * T + qi_ * 128: p_ * T + qi_ * 128 + 128],
                    ctxT_,
                    X[64 * hh_: 64 * hh_ + 64, p_ * T + qi_ * 128: p_ * T + qi_ * 128 + 128]),
                    deps=[o_btr])
                slot_rd[sl] = [o_xa]
                xa_ops.append(o_xa)
            innerA_rd = s_war[0] + s_war[1]
            innerC_rd = [ptev[0]] + ctx_war[0] + xa_ops[-2:]
            innerD_rd = [ptev[1]] + ctx_war[1] + xa_ops[-2:]
            last_xadd += xa_ops[-2:]
            final_exps = s_war[0] + s_war[1]
            final_misc = [ptev[0], ptev[1], resc_last[0]]
            inner_tails[par] = xa_ops[-4:] + final_exps + final_misc
        tails = last_xadd[-4:] + final_exps[-2:] + final_misc
        return [t for t in tails if t is not None]

    tails = list(ld_x) + consts
    do_scr = False
    if 1 in stages:
        g1, t1 = emit_ln(list(ld_x) + consts)
        tails = emit_ffn(g1, t1, consts, scratch="save" if do_scr else None)
    if 2 in stages:
        g2ln, t2 = emit_ln(tails)
        tails = emit_attn(g2ln, t2)
    if 3 in stages:
        g3ln, t3 = emit_ln(tails)
        extra = [scr_sts.get("w1"), scr_sts.get("w2")] if do_scr else []
        tails = emit_ffn(g3ln, t3 + [s for s in extra if s], t3,
                         scratch="load" if do_scr else None)

    st_ops = []
    for h2 in range(2):
        st_ops.append(r.dma(("sync", "act")[h2], lambda e, h2_=h2: e.dma_start(
            out=out_d.rearrange("(c p) t -> p c t", p=128)[:, h2_ * 4:(h2_ + 1) * 4, :],
            in_=X.rearrange("p (c t) -> p c t", t=T)[:, h2_ * 4:(h2_ + 1) * 4, :]),
            deps=tails, chan="xo%d" % h2))
    r.op("sync", lambda e: e.nop(), deps=st_ops)

    with nc.Block() as blk:
        r.emit(blk)
    return nc, es


_CACHE = {}


def _prep(inputs):
    _ = inputs.get("L0_ffn_t_emb")
    f32 = np.float32
    bf = ml_dtypes.bfloat16
    d = {}
    d["sw1"] = np.ascontiguousarray(np.asarray(inputs["L0_ffn_sw1"], f32))
    d["sw3"] = np.ascontiguousarray(np.asarray(inputs["L0_ffn_sw3"], f32))
    d["sw2"] = np.ascontiguousarray(np.asarray(inputs["L0_ffn_sw2"], f32))
    d["w1"] = np.ascontiguousarray(np.asarray(inputs["L0_ffn_w1"], f32))
    d["w2"] = np.ascontiguousarray(np.asarray(inputs["L0_ffn_w2"], f32))
    d["sb1c"] = np.ascontiguousarray(np.asarray(inputs["L0_ffn_sb1"], f32).reshape(16, 128).T)
    d["sb3c"] = np.ascontiguousarray(np.asarray(inputs["L0_ffn_sb3"], f32).reshape(16, 128).T)
    d["sb2r"] = np.asarray(inputs["L0_ffn_sb2"], f32).reshape(1, D)
    d["b1c"] = np.ascontiguousarray(
        np.asarray(inputs["L0_ffn_b1"], f32).reshape(E, 8, 128).transpose(2, 0, 1).reshape(128, 64))
    d["b2r"] = np.asarray(inputs["L0_ffn_b2"], f32).reshape(1, E * D)
    d["wdq"] = np.ascontiguousarray(np.asarray(inputs["L0_attn_W_DQ"], f32))
    d["wuqt"] = np.ascontiguousarray(np.asarray(inputs["L0_attn_W_UQ"], f32).T)
    d["wdkv"] = np.ascontiguousarray(np.asarray(inputs["L0_attn_W_DKV"], f32))
    d["wukvt"] = np.ascontiguousarray(np.asarray(inputs["L0_attn_W_UKV"], f32).T)
    d["wqrt"] = np.ascontiguousarray(np.asarray(inputs["L0_attn_W_QR"], f32).T)
    wkrt_t = np.asarray(inputs["L0_attn_W_KR"], f32).T
    d["wkrt"] = np.ascontiguousarray(np.concatenate([wkrt_t, wkrt_t], axis=0))
    wot_t = np.asarray(inputs["L0_attn_W_O"], f32).T
    d["wot"] = np.ascontiguousarray(np.concatenate([wot_t, wot_t], axis=0))
    knw2 = np.zeros((2, 128), f32)
    knw2[0, 0:64] = np.asarray(inputs["L0_attn_k_norm_w"], f32)
    knw2[1, 64:128] = np.asarray(inputs["L0_attn_k_norm_w"], f32)
    d["knw2"] = knw2
    qnw2 = np.zeros((2, 128), f32)
    qnw2[0, 0:64] = np.asarray(inputs["L0_attn_q_norm_w"], f32)
    qnw2[1, 64:128] = np.asarray(inputs["L0_attn_q_norm_w"], f32)
    d["qnw2"] = qnw2
    d["identb"] = np.eye(128, dtype=bf)
    d["identf"] = np.eye(128, dtype=f32)
    b2o = np.zeros((128, 2), bf)
    b2o[0:64, 0] = 1
    b2o[64:128, 1] = 1
    d["b2ones"] = b2o
    d["onesr"] = np.ones((33, 512), bf)
    d["onesc"] = np.ones((128, 1), bf)
    d["epsln"] = np.full((1, 1), EPS_LN, f32)
    d["epsr2"] = np.full((2, 1), EPS_RMS, f32)
    return d


def kernel(**inputs):
    key = "full"
    if key not in _CACHE:
        _CACHE[key] = build_program()
    nc, _ = _CACHE[key]
    shared = _prep(inputs)
    x = np.asarray(inputs["L0_attn_x"], np.float32)
    in_maps = []
    for i in range(8):
        m = dict(shared)
        m["x"] = np.ascontiguousarray(x[i].T)
        in_maps.append(m)
    res = run_bass_kernel_spmd(nc, in_maps, core_ids=list(range(8)))
    out = np.stack([np.ascontiguousarray(rr["out"].T) for rr in res.results])
    return out.astype(np.float32)
